# version 05
# baseline (speedup 1.0000x reference)
import contextlib

import numpy as np
import ml_dtypes

import concourse.bass as bass
import concourse.mybir as mybir
from concourse.bass_utils import run_bass_kernel_spmd
from concourse.alu_op_type import AluOpType

AF = mybir.ActivationFunctionType
AX = mybir.AxisListType
F32 = mybir.dt.float32
BF16 = mybir.dt.bfloat16
I32 = mybir.dt.int32

C0 = 12582912.0
CINV = 1572864.0
EXPBITS = 0x7F800000
EXPCLAMP = float(0x3C800000)
EXPINV = float(0x7F000000)
INV_SQRT_HS = 0.08838834764831845
ADD, SUB, MUL, MAXOP = AluOpType.add, AluOpType.subtract, AluOpType.mult, AluOpType.max


class Cfg:
    def __init__(self, B, T, D, I, igroup=384, prew=512, upto=9):
        self.B, self.T, self.D, self.I = B, T, D, I
        self.upto = upto
        self.OC = 4
        self.DC = D // 128
        self.IC = I // 128
        self.TQC = T // 128
        self.NQ = self.TQC // 2
        self.NTOK = self.NQ * 128
        self.NHALF = self.NTOK // 2
        self.ncores = 2 * B
        self.G = igroup
        self.GKC = igroup // 128
        self.NGRP = I // igroup
        self.PREW = min(prew, I, D)
        self.chunks = [[j for j in range(self.TQC) if (j // 2) % 2 == m] for m in (0, 1)]
        self.exts = [512 * ((j * 128 + 128 + 511) // 512) for j in self.chunks[0]]
        assert self.exts == [512 * ((j * 128 + 128 + 511) // 512) for j in self.chunks[1]]
        assert self.NHALF <= 512 and T % 512 == 0 and I % igroup == 0


ENGINES = ("PE", "DV", "AC", "GP", "SY")


class Op:
    __slots__ = ("eng", "emit", "idx", "deps", "is_dma", "sem_val", "need_inc", "stream")

    def __init__(self, eng, emit, is_dma, stream=None):
        self.eng, self.emit, self.is_dma = eng, emit, is_dma
        self.stream = stream
        self.idx = -1
        self.deps = []
        self.sem_val = 0
        self.need_inc = is_dma


class Prog:
    def __init__(self):
        self.ops = {e: [] for e in ENGINES}
        self.w = {}
        self.r = {}
        self.scnt = {}

    def op(self, eng, emit, reads=(), writes=(), is_dma=False, after=(), stream=None):
        assert (stream is not None) == is_dma
        o = Op(eng, emit, is_dma, stream)
        o.idx = len(self.ops[eng])
        self.ops[eng].append(o)
        if is_dma:
            self.scnt[stream] = self.scnt.get(stream, 0) + 1
            o.sem_val = 16 * self.scnt[stream]
        deps = list(after)
        for k in reads:
            lw = self.w.get(k)
            if lw is not None:
                deps.append(lw)
            self.r.setdefault(k, []).append(o)
        for k in writes:
            lw = self.w.get(k)
            if lw is not None:
                deps.append(lw)
            deps.extend(self.r.get(k, ()))
            self.r[k] = []
            self.w[k] = o
        best = {}
        dtgt = {}
        for d in deps:
            if d is o:
                continue
            if d.is_dma:
                key = ("D", d.stream)
                tgt = 16 * self.scnt[d.stream] - (16 if (is_dma and stream == d.stream) else 0)
                if key not in dtgt or tgt > dtgt[key][1]:
                    dtgt[key] = (d, tgt)
            else:
                cur = best.get(d.eng)
                if cur is None or d.idx > cur.idx:
                    best[d.eng] = d
        o.deps = list(best.values()) + list(dtgt.values())
        return o

    def barrier(self):
        lasts = [self.ops[e][-1] for e in ENGINES if self.ops[e]]
        newops = []
        for e in ENGINES:
            newops.append(self.op(e, lambda eng: eng.nop(), after=lasts))
        self._fence = newops

    def lower(self, nc, es):
        for e in ENGINES:
            for o in self.ops[e]:
                for d in o.deps:
                    if isinstance(d, tuple):
                        continue
                    if d.eng != o.eng and not d.is_dma:
                        d.need_inc = True
        self.sems = {}
        self.dsems = {}
        for e in ENGINES:
            self.sems[e] = es.enter_context(nc.semaphore(f"sem_{e}"))
            cnt = 0
            for o in self.ops[e]:
                if o.is_dma:
                    if o.stream not in self.dsems:
                        self.dsems[o.stream] = es.enter_context(nc.semaphore(f"dma_{o.stream}"))
                elif o.need_inc:
                    cnt += 1
                    o.sem_val = cnt
                else:
                    o.sem_val = cnt

    def emit_engine(self, ename, eng):
        waited = {}
        last_drain = -1
        for o in self.ops[ename]:
            need_drain = False
            for d in o.deps:
                if isinstance(d, tuple):
                    dd, tgt = d
                    key = ("D", dd.stream)
                    if tgt > waited.get(key, 0):
                        eng.wait_ge(self.dsems[dd.stream], tgt)
                        waited[key] = tgt
                elif d.eng == ename:
                    if d.idx > last_drain:
                        need_drain = True
                else:
                    if d.sem_val > waited.get(d.eng, 0):
                        eng.wait_ge(self.sems[d.eng], d.sem_val)
                        waited[d.eng] = d.sem_val
            if need_drain and ename in ("DV", "AC", "GP"):
                eng.drain()
                last_drain = o.idx - 1
            instr = o.emit(eng)
            if o.is_dma:
                instr.then_inc(self.dsems[o.stream], 16)
            elif o.need_inc:
                instr.then_inc(self.sems[ename], 1)


def build_program(cfg: Cfg):
    nc = bass.Bass("TRN2", target_bir_lowering=False)
    T, D, I = cfg.T, cfg.D, cfg.I
    DC, IC, OC, NQ = cfg.DC, cfg.IC, cfg.OC, cfg.NQ
    NTOK, NHALF = cfg.NTOK, cfg.NHALF
    G, GKC, NGRP = cfg.G, cfg.GKC, cfg.NGRP
    HNT = NHALF

    dram = lambda n, s, d, k="ExternalInput": nc.dram_tensor(n, s, d, kind=k).ap()
    x_seq = dram("x_seq", [T, D], F32)
    x_own = dram("x_own", [NTOK, D], F32)
    wqt = dram("wqt", [D, D], F32)
    wkt = dram("wkt", [D, 512], F32)
    wvt = dram("wvt", [D, 512], F32)
    proj = dram("proj", [D, D], F32)
    gate_i = dram("gate_i", [D, I], I32)
    up_i = dram("up_i", [D, I], I32)
    down_i = dram("down_i", [I, D], I32)
    bq = dram("bq", [128, 16], F32)
    bkv = dram("bkv", [128, 8], F32)
    gsc = dram("gsc", [128, IC], F32)
    usc = dram("usc", [128, IC], F32)
    dsc = dram("dsc", [128, DC], F32)
    masks = dram("masks", [NQ, 128, 512], BF16)
    eyef = dram("eyef", [128, 128], F32)
    eyeb = dram("eyeb", [128, 128], BF16)
    y = dram("y", [NTOK, D], F32, k="ExternalOutput")
    sc_g = dram("sc_g", [DC, 128, I], BF16, k="Internal")
    sc_u = dram("sc_u", [DC, 128, I], BF16, k="Internal")
    sc_d = dram("sc_d", [IC, 128, D], BF16, k="Internal")

    es = contextlib.ExitStack()
    sb = lambda n, s, d: es.enter_context(nc.sbuf_tensor(n, s, d))
    P = Prog()

    IDF = sb("IDF", [128, 128], F32)
    IDB = sb("IDB", [128, 128], BF16)
    ONEC = sb("ONEC", [128, 1], F32)
    ONER = sb("ONER", [1, 128], F32)
    BQR = sb("BQR", [128, 16], F32)
    BQS = sb("BQS", [128, 4], F32)
    BKV = sb("BKV", [128, 8], F32)
    GSC = sb("GSC", [128, IC], F32)
    USC = sb("USC", [128, IC], F32)
    DSC = sb("DSC", [128, DC], F32)

    def ld(dst_ap, src_ap, keys, stream="CONST"):
        P.op("SY", lambda e, d=dst_ap, s=src_ap: e.dma_start(out=d, in_=s),
             writes=keys, is_dma=True, stream=stream)

    for dd, ss_, kk in ((IDF, eyef, "IDF"), (IDB, eyeb, "IDB"), (BQR, bq, "BQR"),
                        (BKV, bkv, "BKV"), (GSC, gsc, "GSC"), (USC, usc, "USC"),
                        (DSC, dsc, "DSC")):
        ld(dd[:], ss_, [kk])
    P.op("GP", lambda e: e.memset(ONEC[:], 1.0), writes=["ONEC"])
    P.op("GP", lambda e: e.memset(ONER[:], 1.0), writes=["ONER"])

    def bqs_emit(e):
        for gg in range(4):
            e.tensor_tensor(BQS[:, gg:gg + 1], BQR[:, gg:gg + 1], BQR[:, gg + 4:gg + 5], ADD)
            e.drain()
            for r in (8, 12):
                e.tensor_tensor(BQS[:, gg:gg + 1], BQS[:, gg:gg + 1],
                                BQR[:, gg + r:gg + r + 1], ADD)
                e.drain()
        return e.nop()
    P.op("DV", bqs_emit, reads=["BQR"], writes=["BQS"])

    W = cfg.PREW
    gate_sl = [("g", gate_i, sc_g, kc, ws) for ws in range(I // W) for kc in range(DC)]
    up_sl = [("u", up_i, sc_u, kc, ws) for ws in range(I // W) for kc in range(DC)]
    down_sl = [("d", down_i, sc_d, kc, ws) for kc in range(IC) for ws in range(D // W)]
    pslices = []
    ng, nu, nd = len(gate_sl), len(up_sl), len(down_sl)
    gi_, ui_, di_ = 0, 0, 0
    for k in range(ng + nu + nd):
        pick = k % 3
        if pick == 0 and gi_ < ng:
            pslices.append(gate_sl[gi_]); gi_ += 1
        elif pick == 1 and ui_ < nu:
            pslices.append(up_sl[ui_]); ui_ += 1
        elif di_ < nd:
            pslices.append(down_sl[di_]); di_ += 1
        elif gi_ < ng:
            pslices.append(gate_sl[gi_]); gi_ += 1
        elif ui_ < nu:
            pslices.append(up_sl[ui_]); ui_ += 1
    NSL = len(pslices)
    assert NSL == ng + nu + nd

    def grp_bound(grp):
        ws_need = ((grp + 1) * G - 1) // W
        icg_max = (grp + 1) * GKC - 1
        bound = 0
        for i, (tag, _, _, kc, ws) in enumerate(pslices):
            need = (tag in ("g", "u") and ws <= ws_need) or (tag == "d" and kc <= icg_max)
            if need:
                bound = i + 1
        return bound
    GRP_BOUND = [grp_bound(g_) for g_ in range(NGRP)]

    UB = [sb(f"UB{i}", [128, W], I32) for i in range(2)]
    SCB = sb("SCB", [128, W], I32)
    SCIB = sb("SCIB", [128, W], I32)
    YB = sb("YB", [128, W], F32)
    IIB = sb("IIB", [128, W], I32)
    Q1B = sb("Q1B", [128, W], F32)
    WQB = [sb(f"WQB{i}", [128, W], BF16) for i in range(2)]
    _pp = [0]

    def emit_prepass(n):
        for _ in range(n):
            i = _pp[0]
            if i >= NSL:
                return
            _pp[0] += 1
            tag, src, dst, kc, ws = pslices[i]
            pu = i % 2
            P.op("SY", lambda e, s=src, kc=kc, ws=ws, p=pu: e.dma_start(
                out=UB[p][:], in_=s[kc * 128:(kc + 1) * 128, ws * W:(ws + 1) * W]),
                writes=[("U", pu)], is_dma=True, stream=f"U{pu}")

            def dv1(e, p=pu):
                e.tensor_single_scalar(SCB[:], UB[p][:], EXPBITS, AluOpType.bitwise_and)
                e.drain()
                e.tensor_single_scalar(SCB[:], SCB[:], EXPCLAMP, MAXOP)
                e.drain()
                return e.tensor_scalar(SCIB[:], SCB[:], -1.0, EXPINV, MUL, ADD)
            P.op("DV", dv1, reads=[("U", pu)], writes=["SCB", "SCIB"])
            P.op("GP", lambda e, p=pu: e.tensor_tensor(
                YB[:], UB[p][:].bitcast(F32), SCIB[:].bitcast(F32), MUL),
                reads=[("U", pu), "SCIB"], writes=["YB"])

            def ac1(e):
                e.activation(IIB[:], YB[:], AF.Copy, bias=C0, scale=8.0)
                e.drain()
                return e.activation(Q1B[:], IIB[:], AF.Copy, bias=-CINV, scale=0.125)
            P.op("AC", ac1, reads=["YB"], writes=["IIB", "Q1B"])
            P.op("DV", lambda e, p=pu: e.tensor_tensor(
                WQB[p][:], Q1B[:], SCB[:].bitcast(F32), MUL),
                reads=["Q1B", "SCB"], writes=[("WQ", pu)])
            P.op("GP", lambda e, d=dst, t=tag, kc=kc, ws=ws, p=pu: e.dma_start(
                out=d[kc][:, ws * W:(ws + 1) * W], in_=WQB[p][:]),
                reads=[("WQ", pu)], writes=[("scr", tag, kc, ws)], is_dma=True,
                stream=f"WQ{pu}")

    def ensure_prepass(bound):
        if _pp[0] < bound:
            emit_prepass(bound - _pp[0])

    UPTO = cfg.upto
    ZDH = sb("ZDH", [128, OC * NTOK], BF16)
    ZDL = sb("ZDL", [128, OC * NTOK], BF16)

    with contextlib.ExitStack() as a_es:
        asb = lambda n, s, d: a_es.enter_context(nc.sbuf_tensor(n, s, d))
        KH = asb("KH", [128, OC * T], BF16)
        KL = asb("KL", [128, OC * T], BF16)
        VTH = asb("VTH", [128, (T // 128) * 512], BF16)
        VTL = asb("VTL", [128, (T // 128) * 512], BF16)
        QH = asb("QH", [128, OC * NTOK], BF16)
        QL = asb("QL", [128, OC * NTOK], BF16)

        SEQ = 512
        NSC = T // SEQ
        NTT = SEQ // 128
        with contextlib.ExitStack() as b2:
            tsb = lambda n, s, d: b2.enter_context(nc.sbuf_tensor(n, s, d))
            tps = lambda n, s: b2.enter_context(nc.psum_tensor(n, s, F32))
            XT = [tsb(f"XT{i}", [128, D], F32) for i in range(2)]
            SQH = tsb("SQH", [128, DC * SEQ], BF16)
            SQL = tsb("SQL", [128, DC * SEQ], BF16)
            WRB = [tsb(f"WRB{i}", [128, 512], F32) for i in range(2)]
            WSH = [tsb(f"WSH{i}", [128, 512], BF16) for i in range(2)]
            WSL = [tsb(f"WSL{i}", [128, 512], BF16) for i in range(2)]
            KTMP = tsb("KTMP", [128, 512], F32)
            VTMP = tsb("VTMP", [128, 512], F32)
            PTR = [tps(f"PTR_{i}", [128, 128]) for i in range(4)]
            PK = [tps(f"PK{i}", [128, 512]) for i in range(4)]

            for sc in range(NSC if UPTO >= 2 else 0):
                for tt in range(NTT):
                    it = sc * NTT + tt
                    P.op("SY", lambda e, r0=sc * SEQ + tt * 128, p=it % 2: e.dma_start(
                        out=XT[p][:], in_=x_seq[r0:r0 + 128, :]),
                        writes=[("XT", it % 2)], is_dma=True, stream=f"XT{it % 2}")
                    for d in range(DC):
                        k = (it * DC + d) % 4
                        P.op("PE", lambda e, p=it % 2, d=d, k=k: e.transpose(
                            PTR[k][:], XT[p][:, d * 128:(d + 1) * 128], IDF[:]),
                            reads=[("XT", it % 2), "IDF"], writes=[("PTR", k)])

                        def spl(e, d=d, tt=tt, k=k):
                            slot = PTR[k][:]
                            dh = SQH[:, d * SEQ + tt * 128: d * SEQ + tt * 128 + 128]
                            dl = SQL[:, d * SEQ + tt * 128: d * SEQ + tt * 128 + 128]
                            e.tensor_copy(dh, slot)
                            e.drain()
                            return e.tensor_tensor(dl, slot, dh, SUB)
                        P.op("DV", spl, reads=[("PTR", k)], writes=[("SQ", tt, d)])
                emit_prepass(2)
                for mat, wsrc in ((0, wkt), (1, wvt)):
                    for d in range(DC):
                        par = d % 2
                        P.op("SY", lambda e, w=wsrc, d=d, p=par: e.dma_start(
                            out=WRB[p][:], in_=w[d * 128:(d + 1) * 128, :]),
                            writes=[("WRB", par)], is_dma=True, stream=f"WRB{par}")

                        def wspl(e, p=par):
                            e.tensor_copy(WSH[p][:], WRB[p][:])
                            e.drain()
                            return e.tensor_tensor(WSL[p][:], WRB[p][:], WSH[p][:], SUB)
                        P.op("DV", wspl, reads=[("WRB", par)], writes=[("WS", par)])

                        def kvmm(e, d=d, p=par, first=(d == 0), last=(d == DC - 1)):
                            rh = SQH[:, d * SEQ: d * SEQ + 512]
                            rl = SQL[:, d * SEQ: d * SEQ + 512]
                            for oc in range(OC):
                                wh = WSH[p][:, oc * 128:(oc + 1) * 128]
                                wl = WSL[p][:, oc * 128:(oc + 1) * 128]
                                e.matmul(PK[oc][:], wh, rh, start=first, stop=False)
                                e.matmul(PK[oc][:], wl, rh, start=False, stop=False)
                                instr = e.matmul(PK[oc][:], wh, rl, start=False, stop=last)
                            return instr
                        P.op("PE", kvmm,
                             reads=[("WS", par)] + [("SQ", tt, d) for tt in range(NTT)],
                             writes=[("PK", oc) for oc in range(OC)])
                    if mat == 0:
                        for oc in range(OC):
                            P.op("AC", lambda e, oc=oc: e.activation(
                                KTMP[:], PK[oc][:], AF.Identity, bias=BKV[:, oc:oc + 1], scale=1.0),
                                reads=[("PK", oc), "BKV"], writes=["KTMP"])

                            def kspl(e, oc=oc, sc=sc):
                                dh = KH[:, oc * T + sc * 512: oc * T + sc * 512 + 512]
                                dl = KL[:, oc * T + sc * 512: oc * T + sc * 512 + 512]
                                e.tensor_copy(dh, KTMP[:])
                                e.drain()
                                return e.tensor_tensor(dl, KTMP[:], dh, SUB)
                            P.op("DV", kspl, reads=["KTMP"], writes=[("KHL", sc, oc)])
                    else:
                        for oc in range(OC):
                            P.op("AC", lambda e, oc=oc: e.activation(
                                VTMP[:], PK[oc][:], AF.Identity, bias=BKV[:, 4 + oc:5 + oc], scale=1.0),
                                reads=[("PK", oc), "BKV"], writes=["VTMP"])
                            for ss in range(NTT):
                                k = ss % 4
                                P.op("PE", lambda e, ss=ss, k=k: e.transpose(
                                    PTR[k][:], VTMP[:, ss * 128:(ss + 1) * 128], IDF[:]),
                                    reads=["VTMP", "IDF"], writes=[("PTR", k)])

                                def vspl(e, oc=oc, sc=sc, ss=ss, k=k):
                                    st = sc * NTT + ss
                                    slot = PTR[k][:]
                                    dh = VTH[:, st * 512 + oc * 128: st * 512 + (oc + 1) * 128]
                                    dl = VTL[:, st * 512 + oc * 128: st * 512 + (oc + 1) * 128]
                                    e.tensor_copy(dh, slot)
                                    e.drain()
                                    return e.tensor_tensor(dl, slot, dh, SUB)
                                P.op("DV", vspl, reads=[("PTR", k)],
                                     writes=[("VT", sc * NTT + ss, oc)])
                emit_prepass(2)
            P.barrier()

        with contextlib.ExitStack() as b3:
            tsb = lambda n, s, d: b3.enter_context(nc.sbuf_tensor(n, s, d))
            tps = lambda n, s: b3.enter_context(nc.psum_tensor(n, s, F32))
            XT3 = [tsb("XT30", [128, D], F32)] * 2
            XSH = tsb("XSH", [128, DC * NHALF], BF16)
            XSL = tsb("XSL", [128, DC * NHALF], BF16)
            WQS = [tsb("WQS0", [128, D], F32)] * 2
            WQH = tsb("WQH", [128, DC * 512], BF16)
            WQL = tsb("WQL", [128, DC * 512], BF16)
            WRED = tsb("WRED", [128, 512], F32)
            QTMP = tsb("QTMP", [128, HNT], F32)
            PTR3 = [tps(f"PTR3_{i}", [128, 128]) for i in range(4)]
            PQ = [tps(f"PQ{i}", [128, HNT]) for i in range(4)]
            NTTH = NHALF // 128

            for d in range(DC if UPTO >= 3 else 0):
                par = 0
                P.op("SY", lambda e, d=d: e.dma_start(
                    out=WQS[0][:], in_=wqt[d * 128:(d + 1) * 128, :]),
                    writes=[("WQS", 0)], is_dma=True, stream="WQS0")

                def wred(e, p=par, d=d):
                    for gg in range(4):
                        dst = WRED[:, gg * 128:(gg + 1) * 128]
                        sl = lambda h: WQS[p][:, (gg + 4 * h) * 128:(gg + 4 * h + 1) * 128]
                        e.tensor_tensor(dst, sl(0), sl(1), ADD)
                        e.drain()
                        e.tensor_tensor(dst, dst, sl(2), ADD)
                        e.drain()
                        e.tensor_tensor(dst, dst, sl(3), ADD)
                        e.drain()
                    e.tensor_copy(WQH[:, d * 512:(d + 1) * 512], WRED[:])
                    e.drain()
                    return e.tensor_tensor(WQL[:, d * 512:(d + 1) * 512], WRED[:],
                                           WQH[:, d * 512:(d + 1) * 512], SUB)
                P.op("DV", wred, reads=[("WQS", 0)], writes=[("WQHL", d), "WRED"])
                if d % 4 == 3:
                    emit_prepass(1)

            for half in range(2 if UPTO >= 3 else 0):
                for tt in range(NTTH):
                    it = half * NTTH + tt
                    P.op("SY", lambda e, r0=it * 128: e.dma_start(
                        out=XT3[0][:], in_=x_own[r0:r0 + 128, :]),
                        writes=[("XT3", 0)], is_dma=True, stream="XT30")
                    for d in range(DC):
                        k = (it * DC + d) % 4
                        P.op("PE", lambda e, d=d, k=k: e.transpose(
                            PTR3[k][:], XT3[0][:, d * 128:(d + 1) * 128], IDF[:]),
                            reads=[("XT3", 0), "IDF"], writes=[("PTR3", k)])

                        def spl3(e, d=d, tt=tt, k=k):
                            slot = PTR3[k][:]
                            dh = XSH[:, d * NHALF + tt * 128: d * NHALF + tt * 128 + 128]
                            dl = XSL[:, d * NHALF + tt * 128: d * NHALF + tt * 128 + 128]
                            e.tensor_copy(dh, slot)
                            e.drain()
                            return e.tensor_tensor(dl, slot, dh, SUB)
                        P.op("DV", spl3, reads=[("PTR3", k)], writes=[("XS", tt, d)])
                emit_prepass(2)

                def qmm(e):
                    for d in range(DC):
                        first, last = d == 0, d == DC - 1
                        for oc in range(OC):
                            wh = WQH[:, d * 512 + oc * 128: d * 512 + (oc + 1) * 128]
                            wl = WQL[:, d * 512 + oc * 128: d * 512 + (oc + 1) * 128]
                            rh = XSH[:, d * NHALF: d * NHALF + HNT]
                            rl = XSL[:, d * NHALF: d * NHALF + HNT]
                            e.matmul(PQ[oc][:], wh, rh, start=first, stop=False)
                            e.matmul(PQ[oc][:], wl, rh, start=False, stop=False)
                            instr = e.matmul(PQ[oc][:], wh, rl, start=False, stop=last)
                    return instr
                P.op("PE", qmm,
                     reads=[("WQHL", d) for d in range(DC)]
                           + [("XS", tt, d) for tt in range(NTTH) for d in range(DC)],
                     writes=[("PQ", oc) for oc in range(OC)])
                for oc in range(OC):
                    P.op("AC", lambda e, oc=oc: e.activation(
                        QTMP[:], PQ[oc][:], AF.Identity, bias=BQS[:, oc:oc + 1], scale=1.0),
                        reads=[("PQ", oc), "BQS"], writes=["QTMP"])

                    def qspl(e, oc=oc, half=half):
                        o0 = oc * NTOK + half * NHALF
                        e.tensor_copy(QH[:, o0:o0 + NHALF], QTMP[:])
                        e.drain()
                        return e.tensor_tensor(QL[:, o0:o0 + NHALF], QTMP[:],
                                               QH[:, o0:o0 + NHALF], SUB)
                    P.op("DV", qspl, reads=["QTMP"], writes=[("Q", half, oc)])
            P.barrier()

        with contextlib.ExitStack() as b4:
            tsb = lambda n, s, d: b4.enter_context(nc.sbuf_tensor(n, s, d))
            tps = lambda n, s: b4.enter_context(nc.psum_tensor(n, s, F32))
            tpsb = lambda n, s: b4.enter_context(nc.psum_tensor(n, s, BF16))
            MSK = tsb("MSK", [128, NQ * 512], BF16)
            for q in range(NQ):
                ld(MSK[:, q * 512:(q + 1) * 512], masks[q], [("MSK", q)], stream="MSKS")
            S32 = tsb("S32", [128, T], F32)
            P32 = tsb("P32", [128, T], F32)
            PHh = tsb("PHh", [128, T], BF16)
            PLl = tsb("PLl", [128, T], BF16)
            RS = tsb("RS", [128, 4], F32)
            MXT = tsb("MXT", [128, 4], F32)
            NEGM = tsb("NEGM", [128, 1], F32)
            RSUM = tsb("RSUM", [128, 1], F32)
            RR = tsb("RR", [128, 1], F32)
            PHT = [tsb(f"PHT{i}", [128, 128], BF16) for i in range(2)]
            PLT = [tsb(f"PLT{i}", [128, 128], BF16) for i in range(2)]
            PSC = [tps(f"PSC{i}", [128, 512]) for i in range(2)]
            ZP = tps("ZP", [128, 512])
            PTP = [tpsb(f"PTP{i}", [128, 256]) for i in range(2)]

            for q in range(NQ if UPTO >= 4 else 0):
                ext = cfg.exts[q]
                ntile = ext // 512
                nsub = ext // 128
                half = q // (NQ // 2)
                for ti in range(ntile):
                    def smm(e, q=q, ti=ti):
                        for oc in range(OC):
                            lh = QH[:, oc * NTOK + q * 128: oc * NTOK + (q + 1) * 128]
                            ll = QL[:, oc * NTOK + q * 128: oc * NTOK + (q + 1) * 128]
                            rh = KH[:, oc * T + ti * 512: oc * T + ti * 512 + 512]
                            rl = KL[:, oc * T + ti * 512: oc * T + ti * 512 + 512]
                            e.matmul(PSC[ti % 2][:], lh, rh, start=oc == 0, stop=False)
                            e.matmul(PSC[ti % 2][:], ll, rh, start=False, stop=False)
                            instr = e.matmul(PSC[ti % 2][:], lh, rl, start=False,
                                             stop=oc == OC - 1)
                        return instr
                    P.op("PE", smm,
                         reads=[("Q", half, oc) for oc in range(OC)]
                               + [("KHL", ti, oc) for oc in range(OC)],
                         writes=[("PSC", ti % 2)])

                    def sevac(e, q=q, ti=ti, last=(ti == ntile - 1)):
                        dst = S32[:, ti * 512:(ti + 1) * 512]
                        e.tensor_copy(dst, PSC[ti % 2][:])
                        e.drain()
                        if last:
                            e.tensor_tensor(dst, dst, MSK[:, q * 512:(q + 1) * 512], ADD)
                            e.drain()
                        return e.tensor_reduce(MXT[:, ti:ti + 1], dst, AX.X, MAXOP)
                    P.op("DV", sevac, reads=[("PSC", ti % 2), ("MSK", q)],
                         writes=[("S32", ti), ("MXT", ti)])

                def stats(e, ntile=ntile):
                    if ntile == 1:
                        return e.tensor_single_scalar(NEGM[:], MXT[:, 0:1], -1.0, MUL)
                    e.tensor_tensor(NEGM[:], MXT[:, 0:1], MXT[:, 1:2], MAXOP)
                    e.drain()
                    for ti in range(2, ntile):
                        e.tensor_tensor(NEGM[:], NEGM[:], MXT[:, ti:ti + 1], MAXOP)
                        e.drain()
                    return e.tensor_single_scalar(NEGM[:], NEGM[:], -1.0, MUL)
                P.op("DV", stats, reads=[("MXT", ti) for ti in range(ntile)], writes=["NEGM"])

                for ti in range(ntile):
                    P.op("AC", lambda e, ti=ti: e.activation(
                        P32[:, ti * 512:(ti + 1) * 512], S32[:, ti * 512:(ti + 1) * 512],
                        AF.Exp, bias=NEGM[:], scale=1.0, accum_out=RS[:, ti:ti + 1]),
                        reads=[("S32", ti), "NEGM"], writes=[("P32", ti), ("RS", ti)])

                def rsum(e, ntile=ntile):
                    if ntile == 1:
                        e.tensor_copy(RSUM[:], RS[:, 0:1])
                        e.drain()
                    else:
                        e.tensor_tensor(RSUM[:], RS[:, 0:1], RS[:, 1:2], ADD)
                        e.drain()
                        for ti in range(2, ntile):
                            e.tensor_tensor(RSUM[:], RSUM[:], RS[:, ti:ti + 1], ADD)
                            e.drain()
                    e.reciprocal(RR[:], RSUM[:])
                    e.drain()
                    return e.tensor_single_scalar(RR[:], RR[:], INV_SQRT_HS, MUL)
                P.op("DV", rsum, reads=[("RS", ti) for ti in range(ntile)],
                     writes=["RR", "RSUM"])

                for ti in range(ntile):
                    def pscale(e, ti=ti):
                        sl = P32[:, ti * 512:(ti + 1) * 512]
                        e.tensor_scalar(sl, sl, RR[:], None, MUL)
                        e.drain()
                        e.tensor_copy(PHh[:, ti * 512:(ti + 1) * 512], sl)
                        e.drain()
                        return e.tensor_tensor(PLl[:, ti * 512:(ti + 1) * 512], sl,
                                               PHh[:, ti * 512:(ti + 1) * 512], SUB)
                    P.op("DV", pscale, reads=[("P32", ti), "RR"], writes=[("PHL", ti)])

                for ss in range(nsub):
                    def ptt(e, ss=ss):
                        e.transpose(PTP[ss % 2][:, 0:128],
                                    PHh[:, ss * 128:(ss + 1) * 128], IDB[:])
                        return e.transpose(PTP[ss % 2][:, 128:256],
                                           PLl[:, ss * 128:(ss + 1) * 128], IDB[:])
                    P.op("PE", ptt, reads=[("PHL", ss // 4), "IDB"], writes=[("PTP", ss % 2)])

                    def ptev(e, ss=ss):
                        e.tensor_copy(PHT[ss % 2][:], PTP[ss % 2][:, 0:128])
                        e.drain()
                        return e.tensor_copy(PLT[ss % 2][:], PTP[ss % 2][:, 128:256])
                    P.op("DV", ptev, reads=[("PTP", ss % 2)], writes=[("PHT", ss % 2)])

                    def pv(e, ss=ss, nsub=nsub):
                        for oc in range(OC):
                            vh = VTH[:, ss * 512 + oc * 128: ss * 512 + (oc + 1) * 128]
                            vl = VTL[:, ss * 512 + oc * 128: ss * 512 + (oc + 1) * 128]
                            fo = oc * 128
                            e.matmul(ZP[:, fo:fo + 128], vh, PHT[ss % 2][:],
                                     start=(ss == 0 and oc == 0), stop=False)
                            e.matmul(ZP[:, fo:fo + 128], vh, PLT[ss % 2][:],
                                     start=False, stop=False)
                            instr = e.matmul(ZP[:, fo:fo + 128], vl, PHT[ss % 2][:],
                                             start=False, stop=(ss == nsub - 1 and oc == OC - 1))
                        return instr
                    P.op("PE", pv,
                         reads=[("PHT", ss % 2)] + [("VT", ss, oc) for oc in range(OC)],
                         writes=["ZPa"])

                def zspl(e, q=q):
                    for oc in range(OC):
                        o0 = oc * NTOK + q * 128
                        e.tensor_copy(ZDH[:, o0:o0 + 128], ZP[:, oc * 128:(oc + 1) * 128])
                        e.drain()
                        e.tensor_tensor(ZDL[:, o0:o0 + 128], ZP[:, oc * 128:(oc + 1) * 128],
                                        ZDH[:, o0:o0 + 128], SUB)
                        e.drain()
                    return e.nop()
                P.op("DV", zspl, reads=["ZPa"], writes=[("ZD", q)])
                emit_prepass(4)
            P.barrier()

    SX = sb("SX", [128, DC * NTOK], F32)

    with contextlib.ExitStack() as b5:
        tsb = lambda n, s, d: b5.enter_context(nc.sbuf_tensor(n, s, d))
        tps = lambda n, s: b5.enter_context(nc.psum_tensor(n, s, F32))
        XT5 = [tsb(f"XT5{i}", [128, D], F32) for i in range(2)]
        PSF = tsb("PSF", [128, D], F32)
        PSH = tsb("PSH", [128, OC * D], BF16)
        PSL = tsb("PSL", [128, OC * D], BF16)
        PRJ = [tsb(f"PRJ{i}", [128, D], F32) for i in range(2)]
        PTR5 = [tps(f"PTR5_{i}", [128, 128]) for i in range(4)]
        PO = [tps(f"PO{i}", [128, HNT]) for i in range(2)]

        for it in range(NTOK // 128 if UPTO >= 5 else 0):
            half = it // (NHALF // 128)
            P.op("SY", lambda e, r0=it * 128, p=it % 2: e.dma_start(
                out=XT5[p][:], in_=x_own[r0:r0 + 128, :]),
                writes=[("XT5", it % 2)], is_dma=True, stream=f"XT5{it % 2}")
            for d in range(DC):
                k = (it * DC + d) % 4
                P.op("PE", lambda e, p=it % 2, d=d, k=k: e.transpose(
                    PTR5[k][:], XT5[p][:, d * 128:(d + 1) * 128], IDF[:]),
                    reads=[("XT5", it % 2), "IDF"], writes=[("PTR5", k)])
                P.op("DV", lambda e, d=d, it=it, k=k: e.tensor_copy(
                    SX[:, d * NTOK + it * 128: d * NTOK + it * 128 + 128],
                    PTR5[k][:]),
                    reads=[("PTR5", k)], writes=[("SX", d, half)])
            if it % 2 == 1:
                emit_prepass(2)

        for gg in range(4 if UPTO >= 5 else 0):
            for r in range(4):
                i = gg * 4 + r
                P.op("SY", lambda e, row=(gg + 4 * r) * 128, p=i % 2: e.dma_start(
                    out=PRJ[p][:], in_=proj[row:row + 128, :]),
                    writes=[("PRJ", i % 2)], is_dma=True, stream=f"PRJ{i % 2}")
                if r == 0:
                    P.op("DV", lambda e, p=i % 2: e.tensor_copy(PSF[:], PRJ[p][:]),
                         reads=[("PRJ", i % 2)], writes=["PSF"])
                else:
                    P.op("DV", lambda e, p=i % 2: e.tensor_tensor(PSF[:], PSF[:], PRJ[p][:], ADD),
                         reads=[("PRJ", i % 2)], writes=["PSF"])

            def pspl(e, gg=gg):
                e.tensor_copy(PSH[:, gg * D:(gg + 1) * D], PSF[:])
                e.drain()
                return e.tensor_tensor(PSL[:, gg * D:(gg + 1) * D], PSF[:],
                                       PSH[:, gg * D:(gg + 1) * D], SUB)
            P.op("DV", pspl, reads=["PSF"], writes=[("PS", gg)])

        for half in range(2 if UPTO >= 5 else 0):
            for d in range(DC):
                def pomm(e, half=half, d=d):
                    for gg in range(4):
                        lh = PSH[:, gg * D + d * 128: gg * D + (d + 1) * 128]
                        llw = PSL[:, gg * D + d * 128: gg * D + (d + 1) * 128]
                        rh = ZDH[:, gg * NTOK + half * NHALF: gg * NTOK + half * NHALF + HNT]
                        rl = ZDL[:, gg * NTOK + half * NHALF: gg * NTOK + half * NHALF + HNT]
                        e.matmul(PO[d % 2][:], lh, rh, start=gg == 0, stop=False)
                        e.matmul(PO[d % 2][:], lh, rl, start=False, stop=False)
                        instr = e.matmul(PO[d % 2][:], llw, rh, start=False, stop=gg == 3)
                    return instr
                P.op("PE", pomm,
                     reads=[("PS", gg) for gg in range(4)]
                           + [("ZD", qq) for qq in range(half * (NQ // 2), (half + 1) * (NQ // 2))],
                     writes=[("PO", d % 2)])
                sl0 = d * NTOK + half * NHALF
                P.op("DV", lambda e, d=d, sl0=sl0: e.tensor_tensor(
                    SX[:, sl0:sl0 + NHALF], PO[d % 2][:], SX[:, sl0:sl0 + NHALF], ADD),
                    reads=[("PO", d % 2)], writes=[("SX", d, half)])
            emit_prepass(2)
        P.barrier()

    def layernorm(tag):
        with contextlib.ExitStack() as lns:
            tsb = lambda n, s, d: lns.enter_context(nc.sbuf_tensor(n, s, d))
            tps = lambda n, s: lns.enter_context(nc.psum_tensor(n, s, F32))
            SQT = [tsb(f"SQT{tag}{i}", [128, NTOK], F32) for i in range(2)]
            NM = tsb(f"NM{tag}", [1, NTOK], F32)
            M2 = tsb(f"M2{tag}", [1, NTOK], F32)
            VAR = tsb(f"VAR{tag}", [1, NTOK], F32)
            SRT = tsb(f"SRT{tag}", [1, NTOK], F32)
            R0 = tsb(f"R0{tag}", [1, NTOK], F32)
            T1 = tsb(f"T1{tag}", [1, NTOK], F32)
            RSTD = tsb(f"RSTD{tag}", [1, NTOK], F32)
            PS1 = tps(f"PS1{tag}", [1, NTOK])
            PS2 = tps(f"PS2{tag}", [1, NTOK])
            PBM = tps(f"PBM{tag}", [128, NTOK])
            PBR = tps(f"PBR{tag}", [128, NTOK])
            NNT = NTOK // HNT
            sxk = [("SX", d, h) for d in range(DC) for h in range(2)]

            def s1mm(e):
                for d in range(DC):
                    for nt in range(NNT):
                        instr = e.matmul(PS1[:, nt * HNT:(nt + 1) * HNT], ONEC[:],
                                         SX[:, d * NTOK + nt * HNT: d * NTOK + nt * HNT + HNT],
                                         start=(d == 0 and (nt * HNT) % 512 == 0),
                                         stop=(d == DC - 1 and (((nt + 1) * HNT) % 512 == 0
                                                                or nt == NNT - 1)))
                return instr
            P.op("PE", s1mm, reads=["ONEC"] + sxk, writes=["PS1" + tag])
            for d in range(DC):
                P.op("AC", lambda e, d=d: e.activation(
                    SQT[d % 2][:], SX[:, d * NTOK:(d + 1) * NTOK], AF.Square),
                    reads=[("SX", d, 0), ("SX", d, 1)], writes=[("SQT" + tag, d % 2)])

                def s2mm(e, d=d):
                    for nt in range(NNT):
                        instr = e.matmul(PS2[:, nt * HNT:(nt + 1) * HNT], ONEC[:],
                                         SQT[d % 2][:, nt * HNT: nt * HNT + HNT],
                                         start=(d == 0 and (nt * HNT) % 512 == 0),
                                         stop=(d == DC - 1 and (((nt + 1) * HNT) % 512 == 0
                                                                or nt == NNT - 1)))
                    return instr
                P.op("PE", s2mm, reads=[("SQT" + tag, d % 2), "ONEC"], writes=["PS2" + tag])

            def small(e):
                e.tensor_single_scalar(NM[:], PS1[:], -1.0 / D, MUL)
                e.drain()
                e.tensor_tensor(M2[:], NM[:], NM[:], MUL)
                e.drain()
                e.scalar_tensor_tensor(VAR[:], PS2[:], 1.0 / D, M2[:], MUL, SUB)
                e.drain()
                return e.tensor_single_scalar(VAR[:], VAR[:], 1e-5, ADD)
            P.op("DV", small, reads=["PS1" + tag, "PS2" + tag],
                 writes=["NM" + tag, "VAR" + tag])
            P.op("AC", lambda e: e.activation(SRT[:], VAR[:], AF.Sqrt),
                 reads=["VAR" + tag], writes=["SRT" + tag])

            def newt(e):
                e.reciprocal(R0[:], SRT[:])
                e.drain()
                e.tensor_tensor(T1[:], R0[:], R0[:], MUL)
                e.drain()
                e.tensor_tensor(T1[:], T1[:], VAR[:], MUL)
                e.drain()
                e.tensor_scalar(T1[:], T1[:], -0.5, 1.5, MUL, ADD)
                e.drain()
                return e.tensor_tensor(RSTD[:], R0[:], T1[:], MUL)
            P.op("DV", newt, reads=["SRT" + tag, "VAR" + tag], writes=["RSTD" + tag])

            def bmm(e):
                for nt in range(NNT):
                    st_ = (nt * HNT) % 512 == 0
                    sp_ = ((nt + 1) * HNT) % 512 == 0 or nt == NNT - 1
                    e.matmul(PBM[:, nt * HNT:(nt + 1) * HNT], ONER[:],
                             NM[:, nt * HNT: nt * HNT + HNT], start=st_, stop=sp_)
                    instr = e.matmul(PBR[:, nt * HNT:(nt + 1) * HNT], ONER[:],
                                     RSTD[:, nt * HNT: nt * HNT + HNT], start=st_, stop=sp_)
                return instr
            P.op("PE", bmm, reads=["NM" + tag, "RSTD" + tag, "ONER"],
                 writes=["PBM" + tag, "PBR" + tag])

            for d in range(DC):
                def apply(e, d=d):
                    for h in range(2):
                        sl = SX[:, d * NTOK + h * NHALF: d * NTOK + h * NHALF + NHALF]
                        e.tensor_tensor(sl, sl, PBM[:, h * NHALF:(h + 1) * NHALF], ADD)
                        e.drain()
                        e.tensor_tensor(sl, sl, PBR[:, h * NHALF:(h + 1) * NHALF], MUL)
                        e.drain()
                    return e.nop()
                P.op("DV", apply, reads=["PBM" + tag, "PBR" + tag],
                     writes=[("SX", d, 0), ("SX", d, 1)])
            P.barrier()

    if UPTO >= 6:
        layernorm("L1")

    with contextlib.ExitStack() as b7:
        tsb = lambda n, s, d: b7.enter_context(nc.sbuf_tensor(n, s, d))
        tps = lambda n, s: b7.enter_context(nc.psum_tensor(n, s, F32))
        XSH7 = tsb("XSH7", [128, DC * NHALF], BF16)
        XSL7 = tsb("XSL7", [128, DC * NHALF], BF16)
        QG = [tsb(f"QG{i}", [128, DC * G], BF16) for i in range(2)]
        QU = [tsb(f"QU{i}", [128, DC * G], BF16) for i in range(2)]
        QD = tsb("QD", [128, GKC * D], BF16)
        G1 = tsb("G1", [128, HNT], F32)
        G1S = tsb("G1S", [128, HNT], F32)
        U1 = tsb("U1", [128, HNT], F32)
        FT = tsb("FT", [128, HNT], F32)
        FH = tsb("FH", [128, GKC * HNT], BF16)
        FL = tsb("FL", [128, GKC * HNT], BF16)
        PG = [tps(f"PG{i}", [128, HNT]) for i in range(2)]
        PU = [tps(f"PU{i}", [128, HNT]) for i in range(2)]
        PD = [tps(f"PD{i}", [128, HNT]) for i in range(2)]

        for half in range(2 if UPTO >= 7 else 0):
            for d in range(DC):
                def xspl(e, d=d, half=half):
                    src = SX[:, d * NTOK + half * NHALF: d * NTOK + half * NHALF + NHALF]
                    e.tensor_copy(XSH7[:, d * NHALF:(d + 1) * NHALF], src)
                    e.drain()
                    return e.tensor_tensor(XSL7[:, d * NHALF:(d + 1) * NHALF], src,
                                           XSH7[:, d * NHALF:(d + 1) * NHALF], SUB)
                P.op("DV", xspl, reads=[("SX", d, half)], writes=[("XS7", d)])

            for grp in range(NGRP):
                par = grp % 2
                if half == 0:
                    ensure_prepass(GRP_BOUND[grp])
                for kc in range(DC):
                    P.op("SY", lambda e, kc=kc, grp=grp, p=par: e.dma_start(
                        out=QG[p][:, kc * G:(kc + 1) * G],
                        in_=sc_g[kc][:, grp * G:(grp + 1) * G]),
                        reads=[("scr", "g", kc, w_) for w_ in
                               range(grp * G // W, min(I // W, ((grp + 1) * G + W - 1) // W))],
                        writes=[("QG", par, kc)], is_dma=True, stream=f"QG{par}")
                    P.op("SY", lambda e, kc=kc, grp=grp, p=par: e.dma_start(
                        out=QU[p][:, kc * G:(kc + 1) * G],
                        in_=sc_u[kc][:, grp * G:(grp + 1) * G]),
                        reads=[("scr", "u", kc, w_) for w_ in
                               range(grp * G // W, min(I // W, ((grp + 1) * G + W - 1) // W))],
                        writes=[("QU", par, kc)], is_dma=True, stream=f"QU{par}")
                for kc in range(GKC):
                    icg = grp * GKC + kc
                    P.op("SY", lambda e, icg=icg, kc=kc: e.dma_start(
                        out=QD[:, kc * D:(kc + 1) * D], in_=sc_d[icg][:, :]),
                        reads=[("scr", "d", icg, w_) for w_ in range(D // W)],
                        writes=[("QD", kc)], is_dma=True, stream="QD")

                for kc in range(GKC):
                    icg = grp * GKC + kc
                    for MAT, PM, QW, SCL, outf, outk in (
                            ("G", PG, QG, GSC, G1, "G1"), ("U", PU, QU, USC, U1, "U1")):
                        def gmm(e, kc=kc, p=par, PM=PM, QW=QW):
                            for d in range(DC):
                                lw = QW[p][:, d * G + kc * 128: d * G + kc * 128 + 128]
                                e.matmul(PM[kc % 2][:], lw, XSH7[:, d * NHALF: d * NHALF + HNT],
                                         start=d == 0, stop=False)
                                instr = e.matmul(PM[kc % 2][:], lw,
                                                 XSL7[:, d * NHALF: d * NHALF + HNT],
                                                 start=False, stop=d == DC - 1)
                            return instr
                        P.op("PE", gmm,
                             reads=[("Q" + MAT, par, d) for d in range(DC)]
                                   + [("XS7", d) for d in range(DC)],
                             writes=[("P" + MAT, kc % 2)])
                        P.op("DV", lambda e, kc=kc, icg=icg, PM=PM, SCL=SCL, outf=outf:
                             e.tensor_scalar(outf[:], PM[kc % 2][:], SCL[:, icg:icg + 1],
                                             None, MUL),
                             reads=[("P" + MAT, kc % 2),
                                    "GSC" if MAT == "G" else "USC"],
                             writes=[outk])
                    P.op("AC", lambda e: e.activation(G1S[:], G1[:], AF.Silu),
                         reads=["G1"], writes=["G1S"])

                    def fuse(e, kc=kc):
                        e.tensor_tensor(FT[:], G1S[:], U1[:], MUL)
                        e.drain()
                        e.tensor_copy(FH[:, kc * HNT:(kc + 1) * HNT], FT[:])
                        e.drain()
                        return e.tensor_tensor(FL[:, kc * HNT:(kc + 1) * HNT], FT[:],
                                               FH[:, kc * HNT:(kc + 1) * HNT], SUB)
                    P.op("DV", fuse, reads=["G1S", "U1"], writes=[("F", kc), "FT"])

                for h in range(DC):
                    def dmm(e, h=h):
                        for kc in range(GKC):
                            lw = QD[:, kc * D + h * 128: kc * D + (h + 1) * 128]
                            e.matmul(PD[h % 2][:], lw, FH[:, kc * HNT:(kc + 1) * HNT],
                                     start=kc == 0, stop=False)
                            instr = e.matmul(PD[h % 2][:], lw, FL[:, kc * HNT:(kc + 1) * HNT],
                                             start=False, stop=kc == GKC - 1)
                        return instr
                    P.op("PE", dmm,
                         reads=[("QD", kc) for kc in range(GKC)]
                               + [("F", kc) for kc in range(GKC)],
                         writes=[("PD", h % 2)])
                    sl0 = h * NTOK + half * NHALF
                    P.op("DV", lambda e, h=h, sl0=sl0: e.scalar_tensor_tensor(
                        SX[:, sl0:sl0 + NHALF], PD[h % 2][:], DSC[:, h:h + 1],
                        SX[:, sl0:sl0 + NHALF], MUL, ADD),
                        reads=[("PD", h % 2), "DSC"], writes=[("SX", h, half)])
        P.barrier()

    if UPTO >= 7:
        assert _pp[0] == NSL, f"prepass incomplete: {_pp[0]}/{NSL}"

    if UPTO >= 8:
        layernorm("L2")

    with contextlib.ExitStack() as b8:
        tsb = lambda n, s, d: b8.enter_context(nc.sbuf_tensor(n, s, d))
        tps = lambda n, s: b8.enter_context(nc.psum_tensor(n, s, F32))
        OUTT = [tsb(f"OUTT{i}", [128, D], F32) for i in range(2)]
        PTR8 = [tps(f"PTR8_{i}", [128, 128]) for i in range(4)]
        for it in range(NTOK // 128 if UPTO >= 9 else 0):
            half = it // (NHALF // 128)
            for d in range(DC):
                k = (it * DC + d) % 4
                P.op("PE", lambda e, it=it, d=d, k=k: e.transpose(
                    PTR8[k][:],
                    SX[:, d * NTOK + it * 128: d * NTOK + it * 128 + 128], IDF[:]),
                    reads=[("SX", d, half), "IDF"], writes=[("PTR8", k)])
                P.op("DV", lambda e, it=it, d=d, k=k: e.tensor_copy(
                    OUTT[it % 2][:, d * 128:(d + 1) * 128], PTR8[k][:]),
                    reads=[("PTR8", k)], writes=[("OUT", it % 2, d)])
            P.op("GP", lambda e, it=it: e.dma_start(
                out=y[it * 128:(it + 1) * 128, :], in_=OUTT[it % 2][:]),
                reads=[("OUT", it % 2, d) for d in range(DC)],
                writes=[("ydone", it)], is_dma=True, stream="YOUT")
        if UPTO >= 9:
            P.op("GP", lambda e: e.nop(), reads=[("ydone", it) for it in range(NTOK // 128)])

    P.lower(nc, es)
    with nc.Block() as blk:
        @blk.tensor
        def _(eng):
            P.emit_engine("PE", eng)

        @blk.vector
        def _(eng):
            P.emit_engine("DV", eng)

        @blk.scalar
        def _(eng):
            P.emit_engine("AC", eng)

        @blk.gpsimd
        def _(eng):
            P.emit_engine("GP", eng)

        @blk.sync
        def _(eng):
            P.emit_engine("SY", eng)

    es.close()
    return nc


def make_core_inputs(cfg, X, WQ, bQ, WK, bK, WV, bV, PRJ, GW, UW, DW, GS, US, DS):
    NQ = cfg.NQ
    wqt = np.ascontiguousarray(WQ.T)
    wkt = np.ascontiguousarray(WK.T)
    wvt = np.ascontiguousarray(WV.T)
    prj = np.ascontiguousarray(PRJ)
    gi = np.ascontiguousarray(GW).view(np.int32)
    ui = np.ascontiguousarray(UW).view(np.int32)
    di = np.ascontiguousarray(DW).view(np.int32)
    bq128 = np.ascontiguousarray(bQ.reshape(16, 128).T)
    bkv = np.ascontiguousarray(
        np.concatenate([bK.reshape(4, 128).T, bV.reshape(4, 128).T], axis=1))
    gsc = np.ascontiguousarray(GS.reshape(cfg.IC, 128).T)
    usc = np.ascontiguousarray(US.reshape(cfg.IC, 128).T)
    dsc = np.ascontiguousarray(DS.reshape(cfg.DC, 128).T)
    eyef = np.eye(128, dtype=np.float32)
    eyeb = np.eye(128, dtype=np.float32).astype(ml_dtypes.bfloat16)

    maps = []
    for core in range(cfg.ncores):
        b, m = core // 2, core % 2
        chunks = cfg.chunks[m]
        rows = np.concatenate([np.arange(j * 128, (j + 1) * 128) for j in chunks])
        mk = np.zeros((NQ, 128, 512), dtype=np.float32)
        for lq, j in enumerate(chunks):
            s0 = cfg.exts[lq] - 512
            col = s0 + np.arange(512)[None, :]
            row = j * 128 + np.arange(128)[:, None]
            mk[lq] = np.where(col <= row, 0.0, -30000.0)
        maps.append({
            "x_seq": np.ascontiguousarray(X[b]),
            "x_own": np.ascontiguousarray(X[b][rows]),
            "wqt": wqt, "wkt": wkt, "wvt": wvt, "proj": prj,
            "gate_i": gi, "up_i": ui, "down_i": di,
            "bq": bq128, "bkv": bkv, "gsc": gsc, "usc": usc, "dsc": dsc,
            "masks": mk.astype(ml_dtypes.bfloat16),
            "eyef": eyef, "eyeb": eyeb,
        })
    return maps


def assemble_output(cfg, results):
    Y = np.zeros((cfg.B, cfg.T, cfg.D), dtype=np.float32)
    for core, res in enumerate(results):
        b, m = core // 2, core % 2
        yc = np.asarray(res["y"])
        for lq, j in enumerate(cfg.chunks[m]):
            Y[b, j * 128:(j + 1) * 128] = yc[lq * 128:(lq + 1) * 128]
    return Y


def kernel(L0_attn_x, L0_attn_WQ, L0_attn_bQ, L0_attn_WK, L0_attn_bK,
           L0_attn_WV, L0_attn_bV, L0_attn_proj, L0_ffn_gate_w, L0_ffn_up_w,
           L0_ffn_down_w, L0_ffn_gate_scale, L0_ffn_up_scale, L0_ffn_down_scale):
    cfg = Cfg(B=4, T=2048, D=2048, I=6144, igroup=384, prew=512)
    X = np.asarray(L0_attn_x, dtype=np.float32)
    maps = make_core_inputs(
        cfg, X,
        np.asarray(L0_attn_WQ, np.float32), np.asarray(L0_attn_bQ, np.float32),
        np.asarray(L0_attn_WK, np.float32), np.asarray(L0_attn_bK, np.float32),
        np.asarray(L0_attn_WV, np.float32), np.asarray(L0_attn_bV, np.float32),
        np.asarray(L0_attn_proj, np.float32),
        np.asarray(L0_ffn_gate_w, np.float32), np.asarray(L0_ffn_up_w, np.float32),
        np.asarray(L0_ffn_down_w, np.float32),
        np.asarray(L0_ffn_gate_scale, np.float32), np.asarray(L0_ffn_up_scale, np.float32),
        np.asarray(L0_ffn_down_scale, np.float32))
    nc = build_program(cfg)
    res = run_bass_kernel_spmd(nc, maps, core_ids=list(range(cfg.ncores)))
    return assemble_output(cfg, res.results)


# version 07
# speedup vs baseline: 1.3345x; 1.3345x over previous
import contextlib

import numpy as np
import ml_dtypes

import concourse.bass as bass
import concourse.mybir as mybir
from concourse.bass_utils import run_bass_kernel_spmd
from concourse.alu_op_type import AluOpType

AF = mybir.ActivationFunctionType
AX = mybir.AxisListType
F32 = mybir.dt.float32
BF16 = mybir.dt.bfloat16
I32 = mybir.dt.int32

C0 = 12582912.0
CINV = 1572864.0
EXPBITS = 0x7F800000
EXPCLAMP = float(0x3C800000)
EXPINV = float(0x7F000000)
INV_SQRT_HS = 0.08838834764831845
ADD, SUB, MUL, MAXOP = AluOpType.add, AluOpType.subtract, AluOpType.mult, AluOpType.max


class Cfg:
    def __init__(self, B, T, D, I, igroup=384, prew=512, upto=9):
        self.B, self.T, self.D, self.I = B, T, D, I
        self.upto = upto
        self.OC = 4
        self.DC = D // 128
        self.IC = I // 128
        self.TQC = T // 128
        self.NQ = self.TQC // 2
        self.NTOK = self.NQ * 128
        self.NHALF = self.NTOK // 2
        self.ncores = 2 * B
        self.G = igroup
        self.GKC = igroup // 128
        self.NGRP = I // igroup
        self.PREW = min(prew, I, D)
        self.chunks = [[j for j in range(self.TQC) if (j // 2) % 2 == m] for m in (0, 1)]
        self.exts = [512 * ((j * 128 + 128 + 511) // 512) for j in self.chunks[0]]
        assert self.exts == [512 * ((j * 128 + 128 + 511) // 512) for j in self.chunks[1]]
        assert self.NHALF <= 512 and T % 512 == 0 and I % igroup == 0


ENGINES = ("PE", "DV", "AC", "GP", "SY")


class Op:
    __slots__ = ("eng", "emit", "idx", "deps", "is_dma", "sem_val", "need_inc", "stream")

    def __init__(self, eng, emit, is_dma, stream=None):
        self.eng, self.emit, self.is_dma = eng, emit, is_dma
        self.stream = stream
        self.idx = -1
        self.deps = []
        self.sem_val = 0
        self.need_inc = is_dma


class Prog:
    def __init__(self):
        self.ops = {e: [] for e in ENGINES}
        self.w = {}
        self.r = {}
        self.scnt = {}

    def op(self, eng, emit, reads=(), writes=(), is_dma=False, after=(), stream=None):
        assert (stream is not None) == is_dma
        o = Op(eng, emit, is_dma, stream)
        o.idx = len(self.ops[eng])
        self.ops[eng].append(o)
        if is_dma:
            self.scnt[stream] = self.scnt.get(stream, 0) + 1
            o.sem_val = 16 * self.scnt[stream]
        deps = list(after)
        for k in reads:
            lw = self.w.get(k)
            if lw is not None:
                deps.append(lw)
            self.r.setdefault(k, []).append(o)
        for k in writes:
            lw = self.w.get(k)
            if lw is not None:
                deps.append(lw)
            deps.extend(self.r.get(k, ()))
            self.r[k] = []
            self.w[k] = o
        best = {}
        dtgt = {}
        for d in deps:
            if d is o:
                continue
            if d.is_dma:
                key = ("D", d.stream)
                tgt = 16 * self.scnt[d.stream] - (16 if (is_dma and stream == d.stream) else 0)
                if key not in dtgt or tgt > dtgt[key][1]:
                    dtgt[key] = (d, tgt)
            else:
                cur = best.get(d.eng)
                if cur is None or d.idx > cur.idx:
                    best[d.eng] = d
        o.deps = list(best.values()) + list(dtgt.values())
        return o

    def barrier(self):
        lasts = [self.ops[e][-1] for e in ENGINES if self.ops[e]]
        def _fence_emit(eng):
            eng.drain()
            return eng.nop()
        for e in ENGINES:
            self.op(e, _fence_emit, after=lasts)

    def lower(self, nc, es):
        for e in ENGINES:
            for o in self.ops[e]:
                for d in o.deps:
                    if isinstance(d, tuple):
                        continue
                    if d.eng != o.eng and not d.is_dma:
                        d.need_inc = True
        self.sems = {}
        self.dsems = {}
        for e in ENGINES:
            self.sems[e] = es.enter_context(nc.semaphore(f"sem_{e}"))
            cnt = 0
            for o in self.ops[e]:
                if o.is_dma:
                    if o.stream not in self.dsems:
                        self.dsems[o.stream] = es.enter_context(nc.semaphore(f"dma_{o.stream}"))
                elif o.need_inc:
                    cnt += 1
                    o.sem_val = cnt
                else:
                    o.sem_val = cnt

    def emit_engine(self, ename, eng):
        waited = {}
        last_drain = -1
        for o in self.ops[ename]:
            need_drain = False
            for d in o.deps:
                if isinstance(d, tuple):
                    dd, tgt = d
                    key = ("D", dd.stream)
                    if tgt > waited.get(key, 0):
                        eng.wait_ge(self.dsems[dd.stream], tgt)
                        waited[key] = tgt
                elif d.eng == ename:
                    if d.idx > last_drain:
                        need_drain = True
                else:
                    if d.sem_val > waited.get(d.eng, 0):
                        eng.wait_ge(self.sems[d.eng], d.sem_val)
                        waited[d.eng] = d.sem_val
            if need_drain and ename in ("DV", "AC", "GP"):
                eng.drain()
                last_drain = o.idx - 1
            instr = o.emit(eng)
            if o.is_dma:
                instr.then_inc(self.dsems[o.stream], 16)
            elif o.need_inc:
                instr.then_inc(self.sems[ename], 1)


def build_program(cfg: Cfg):
    nc = bass.Bass("TRN2", target_bir_lowering=False)
    T, D, I = cfg.T, cfg.D, cfg.I
    DC, IC, OC, NQ = cfg.DC, cfg.IC, cfg.OC, cfg.NQ
    NTOK, NHALF = cfg.NTOK, cfg.NHALF
    G, GKC, NGRP = cfg.G, cfg.GKC, cfg.NGRP
    HNT = NHALF

    dram = lambda n, s, d, k="ExternalInput": nc.dram_tensor(n, s, d, kind=k).ap()
    x_seq = dram("x_seq", [T, D], F32)
    x_own = dram("x_own", [NTOK, D], F32)
    wqt = dram("wqt", [D, D], F32)
    wkt = dram("wkt", [D, 512], F32)
    wvt = dram("wvt", [D, 512], F32)
    proj = dram("proj", [D, D], F32)
    gate_i = dram("gate_i", [D, I], I32)
    up_i = dram("up_i", [D, I], I32)
    down_i = dram("down_i", [I, D], I32)
    bq = dram("bq", [128, 16], F32)
    bkv = dram("bkv", [128, 8], F32)
    gsc = dram("gsc", [128, IC], F32)
    usc = dram("usc", [128, IC], F32)
    dsc = dram("dsc", [128, DC], F32)
    masks = dram("masks", [NQ, 128, 512], BF16)
    eyef = dram("eyef", [128, 128], F32)
    eyeb = dram("eyeb", [128, 128], BF16)
    y = dram("y", [NTOK, D], F32, k="ExternalOutput")
    sc_g = dram("sc_g", [DC, 128, I], BF16, k="Internal")
    sc_u = dram("sc_u", [DC, 128, I], BF16, k="Internal")
    sc_d = dram("sc_d", [IC, 128, D], BF16, k="Internal")

    es = contextlib.ExitStack()
    sb = lambda n, s, d: es.enter_context(nc.sbuf_tensor(n, s, d))
    P = Prog()

    IDF = sb("IDF", [128, 128], F32)
    IDB = sb("IDB", [128, 128], BF16)
    ONEC = sb("ONEC", [128, 1], F32)
    ONER = sb("ONER", [1, 128], F32)
    BQR = sb("BQR", [128, 16], F32)
    BQS = sb("BQS", [128, 4], F32)
    BKV = sb("BKV", [128, 8], F32)
    GSC = sb("GSC", [128, IC], F32)
    USC = sb("USC", [128, IC], F32)
    DSC = sb("DSC", [128, DC], F32)

    def ld(dst_ap, src_ap, keys, stream="CONST"):
        P.op("SY", lambda e, d=dst_ap, s=src_ap: e.dma_start(out=d, in_=s),
             writes=keys, is_dma=True, stream=stream)

    for dd, ss_, kk in ((IDF, eyef, "IDF"), (IDB, eyeb, "IDB"), (BQR, bq, "BQR"),
                        (BKV, bkv, "BKV"), (GSC, gsc, "GSC"), (USC, usc, "USC"),
                        (DSC, dsc, "DSC")):
        ld(dd[:], ss_, [kk])
    P.op("GP", lambda e: e.memset(ONEC[:], 1.0), writes=["ONEC"])
    P.op("GP", lambda e: e.memset(ONER[:], 1.0), writes=["ONER"])

    def bqs_emit(e):
        for gg in range(4):
            e.tensor_tensor(BQS[:, gg:gg + 1], BQR[:, gg:gg + 1], BQR[:, gg + 4:gg + 5], ADD)
            e.drain()
            for r in (8, 12):
                e.tensor_tensor(BQS[:, gg:gg + 1], BQS[:, gg:gg + 1],
                                BQR[:, gg + r:gg + r + 1], ADD)
                e.drain()
        return e.nop()
    P.op("DV", bqs_emit, reads=["BQR"], writes=["BQS"])

    W = cfg.PREW
    gate_sl = [("g", gate_i, sc_g, kc, ws) for ws in range(I // W) for kc in range(DC)]
    up_sl = [("u", up_i, sc_u, kc, ws) for ws in range(I // W) for kc in range(DC)]
    down_sl = [("d", down_i, sc_d, kc, ws) for kc in range(IC) for ws in range(D // W)]
    pslices = []
    ng, nu, nd = len(gate_sl), len(up_sl), len(down_sl)
    gi_, ui_, di_ = 0, 0, 0
    for k in range(ng + nu + nd):
        pick = k % 3
        if pick == 0 and gi_ < ng:
            pslices.append(gate_sl[gi_]); gi_ += 1
        elif pick == 1 and ui_ < nu:
            pslices.append(up_sl[ui_]); ui_ += 1
        elif di_ < nd:
            pslices.append(down_sl[di_]); di_ += 1
        elif gi_ < ng:
            pslices.append(gate_sl[gi_]); gi_ += 1
        elif ui_ < nu:
            pslices.append(up_sl[ui_]); ui_ += 1
    NSL = len(pslices)
    assert NSL == ng + nu + nd

    def grp_bound(grp):
        ws_need = ((grp + 1) * G - 1) // W
        icg_max = (grp + 1) * GKC - 1
        bound = 0
        for i, (tag, _, _, kc, ws) in enumerate(pslices):
            need = (tag in ("g", "u") and ws <= ws_need) or (tag == "d" and kc <= icg_max)
            if need:
                bound = i + 1
        return bound
    GRP_BOUND = [grp_bound(g_) for g_ in range(NGRP)]

    def ensure_prepass(bound):
        pass

    UPTO = cfg.upto
    ZDH = sb("ZDH", [128, OC * NTOK], BF16)
    ZDL = sb("ZDL", [128, OC * NTOK], BF16)

    with contextlib.ExitStack() as a_es:
        asb = lambda n, s, d: a_es.enter_context(nc.sbuf_tensor(n, s, d))
        KH = asb("KH", [128, OC * T], BF16)
        KL = asb("KL", [128, OC * T], BF16)
        VTH = asb("VTH", [128, (T // 128) * 512], BF16)
        VTL = asb("VTL", [128, (T // 128) * 512], BF16)
        QH = asb("QH", [128, OC * NTOK], BF16)
        QL = asb("QL", [128, OC * NTOK], BF16)
        psb = lambda n, s, d: a_es.enter_context(nc.sbuf_tensor(n, s, d))
        UB = [psb(f"UB{i}", [128, W], I32) for i in range(2)]
        SCB = [psb(f"SCB{i}", [128, W], I32) for i in range(2)]
        SCIB = [psb(f"SCIB{i}", [128, W], I32) for i in range(2)]
        YB = [psb(f"YB{i}", [128, W], F32) for i in range(2)]
        IIB = [psb(f"IIB{i}", [128, W], I32) for i in range(2)]
        Q1B = [psb(f"Q1B{i}", [128, W], F32) for i in range(2)]
        WQB = [psb(f"WQB{i}", [128, W], BF16) for i in range(2)]
        _pp = [0]
        _last_scr = {}

        def _prepass_front(i):
            tag, src_, dst, kc, ws = pslices[i]
            p = i % 2
            P.op("SY", lambda e, s=src_, kc=kc, ws=ws, p=p: e.dma_start(
                out=UB[p][:], in_=s[kc * 128:(kc + 1) * 128, ws * W:(ws + 1) * W]),
                writes=[("U", p)], is_dma=True, stream=f"U{p}")

            def dv1(e, p=p):
                e.tensor_single_scalar(SCB[p][:], UB[p][:], EXPBITS, AluOpType.bitwise_and)
                e.drain()
                e.tensor_single_scalar(SCB[p][:], SCB[p][:], EXPCLAMP, MAXOP)
                e.drain()
                return e.tensor_scalar(SCIB[p][:], SCB[p][:], -1.0, EXPINV, MUL, ADD)
            P.op("DV", dv1, reads=[("U", p)], writes=[("SC", p), ("SCI", p)])
            P.op("GP", lambda e, p=p: e.tensor_tensor(
                YB[p][:], UB[p][:].bitcast(F32), SCIB[p][:].bitcast(F32), MUL),
                reads=[("U", p), ("SCI", p)], writes=[("Y", p)])

            def ac1(e, p=p):
                e.activation(IIB[p][:], YB[p][:], AF.Copy, bias=C0, scale=8.0)
                e.drain()
                return e.activation(Q1B[p][:], IIB[p][:], AF.Copy, bias=-CINV, scale=0.125)
            P.op("AC", ac1, reads=[("Y", p)], writes=[("II", p), ("Q1", p)])

        def _prepass_back(i):
            tag, src_, dst, kc, ws = pslices[i]
            p = i % 2
            P.op("GP", lambda e, p=p: e.tensor_tensor(
                WQB[p][:], Q1B[p][:], SCB[p][:].bitcast(F32), MUL),
                reads=[("Q1", p), ("SC", p)], writes=[("WQ", p)])
            P.op("AC", lambda e, d=dst, kc=kc, ws=ws, p=p: e.dma_start(
                out=d[kc][:, ws * W:(ws + 1) * W], in_=WQB[p][:]),
                reads=[("WQ", p)], writes=[("scr", tag, kc, ws)], is_dma=True,
                stream=f"WQ{p}")
            _last_scr[p] = ("scr", tag, kc, ws)

        def emit_prepass(n):
            for _ in range(n):
                i = _pp[0]
                if i >= NSL:
                    return
                _pp[0] += 1
                _prepass_front(i)
                if i >= 1:
                    _prepass_back(i - 1)
                if i == NSL - 1:
                    _prepass_back(i)



        SEQ = 512
        NSC = T // SEQ
        NTT = SEQ // 128
        with contextlib.ExitStack() as b2:
            tsb = lambda n, s, d: b2.enter_context(nc.sbuf_tensor(n, s, d))
            tps = lambda n, s: b2.enter_context(nc.psum_tensor(n, s, F32))
            XT = [tsb(f"XT{i}", [128, D], F32) for i in range(2)]
            SQH = tsb("SQH", [128, DC * SEQ], BF16)
            SQL = tsb("SQL", [128, DC * SEQ], BF16)
            WRB = [tsb(f"WRB{i}", [128, 512], F32) for i in range(2)]
            WSH = [tsb(f"WSH{i}", [128, 512], BF16) for i in range(2)]
            WSL = [tsb(f"WSL{i}", [128, 512], BF16) for i in range(2)]
            KTMP = tsb("KTMP", [128, 512], F32)
            VTMP = tsb("VTMP", [128, 512], F32)
            PTR = [tps(f"PTR_{i}", [128, 128]) for i in range(4)]
            PK = [tps(f"PK{i}", [128, 512]) for i in range(4)]

            for sc in range(NSC if UPTO >= 2 else 0):
                for tt in range(NTT):
                    it = sc * NTT + tt
                    P.op("SY", lambda e, r0=sc * SEQ + tt * 128, p=it % 2: e.dma_start(
                        out=XT[p][:], in_=x_seq[r0:r0 + 128, :]),
                        writes=[("XT", it % 2)], is_dma=True, stream=f"XT{it % 2}")
                    for d in range(DC):
                        k = (it * DC + d) % 4
                        P.op("PE", lambda e, p=it % 2, d=d, k=k: e.transpose(
                            PTR[k][:], XT[p][:, d * 128:(d + 1) * 128], IDF[:]),
                            reads=[("XT", it % 2), "IDF"], writes=[("PTR", k)])

                        def spl(e, d=d, tt=tt, k=k):
                            slot = PTR[k][:]
                            dh = SQH[:, d * SEQ + tt * 128: d * SEQ + tt * 128 + 128]
                            dl = SQL[:, d * SEQ + tt * 128: d * SEQ + tt * 128 + 128]
                            e.tensor_copy(dh, slot)
                            e.drain()
                            return e.tensor_tensor(dl, slot, dh, SUB)
                        P.op("DV", spl, reads=[("PTR", k)], writes=[("SQ", tt, d)])
                emit_prepass(18)
                for mat, wsrc in ((0, wkt), (1, wvt)):
                    for d in range(DC):
                        par = d % 2
                        P.op("SY", lambda e, w=wsrc, d=d, p=par: e.dma_start(
                            out=WRB[p][:], in_=w[d * 128:(d + 1) * 128, :]),
                            writes=[("WRB", par)], is_dma=True, stream=f"WRB{par}")

                        def wspl(e, p=par):
                            e.tensor_copy(WSH[p][:], WRB[p][:])
                            e.drain()
                            return e.tensor_tensor(WSL[p][:], WRB[p][:], WSH[p][:], SUB)
                        P.op("DV", wspl, reads=[("WRB", par)], writes=[("WS", par)])

                        def kvmm(e, d=d, p=par, first=(d == 0), last=(d == DC - 1)):
                            rh = SQH[:, d * SEQ: d * SEQ + 512]
                            rl = SQL[:, d * SEQ: d * SEQ + 512]
                            for oc in range(OC):
                                wh = WSH[p][:, oc * 128:(oc + 1) * 128]
                                wl = WSL[p][:, oc * 128:(oc + 1) * 128]
                                e.matmul(PK[oc][:], wh, rh, start=first, stop=False)
                                e.matmul(PK[oc][:], wl, rh, start=False, stop=False)
                                instr = e.matmul(PK[oc][:], wh, rl, start=False, stop=last)
                            return instr
                        P.op("PE", kvmm,
                             reads=[("WS", par)] + [("SQ", tt, d) for tt in range(NTT)],
                             writes=[("PK", oc) for oc in range(OC)])
                    if mat == 0:
                        for oc in range(OC):
                            P.op("AC", lambda e, oc=oc: e.activation(
                                KTMP[:], PK[oc][:], AF.Identity, bias=BKV[:, oc:oc + 1], scale=1.0),
                                reads=[("PK", oc), "BKV"], writes=["KTMP"])

                            def kspl(e, oc=oc, sc=sc):
                                dh = KH[:, oc * T + sc * 512: oc * T + sc * 512 + 512]
                                dl = KL[:, oc * T + sc * 512: oc * T + sc * 512 + 512]
                                e.tensor_copy(dh, KTMP[:])
                                e.drain()
                                return e.tensor_tensor(dl, KTMP[:], dh, SUB)
                            P.op("DV", kspl, reads=["KTMP"], writes=[("KHL", sc, oc)])
                    else:
                        for oc in range(OC):
                            P.op("AC", lambda e, oc=oc: e.activation(
                                VTMP[:], PK[oc][:], AF.Identity, bias=BKV[:, 4 + oc:5 + oc], scale=1.0),
                                reads=[("PK", oc), "BKV"], writes=["VTMP"])
                            for ss in range(NTT):
                                k = ss % 4
                                P.op("PE", lambda e, ss=ss, k=k: e.transpose(
                                    PTR[k][:], VTMP[:, ss * 128:(ss + 1) * 128], IDF[:]),
                                    reads=["VTMP", "IDF"], writes=[("PTR", k)])

                                def vspl(e, oc=oc, sc=sc, ss=ss, k=k):
                                    st = sc * NTT + ss
                                    slot = PTR[k][:]
                                    dh = VTH[:, st * 512 + oc * 128: st * 512 + (oc + 1) * 128]
                                    dl = VTL[:, st * 512 + oc * 128: st * 512 + (oc + 1) * 128]
                                    e.tensor_copy(dh, slot)
                                    e.drain()
                                    return e.tensor_tensor(dl, slot, dh, SUB)
                                P.op("DV", vspl, reads=[("PTR", k)],
                                     writes=[("VT", sc * NTT + ss, oc)])
                emit_prepass(18)
            P.barrier()

        with contextlib.ExitStack() as b3:
            tsb = lambda n, s, d: b3.enter_context(nc.sbuf_tensor(n, s, d))
            tps = lambda n, s: b3.enter_context(nc.psum_tensor(n, s, F32))
            XT3 = [tsb("XT30", [128, D], F32)] * 2
            XSH = tsb("XSH", [128, DC * NHALF], BF16)
            XSL = tsb("XSL", [128, DC * NHALF], BF16)
            WQS = [tsb("WQS0", [128, D], F32)] * 2
            WQH = tsb("WQH", [128, DC * 512], BF16)
            WQL = tsb("WQL", [128, DC * 512], BF16)
            WRED = tsb("WRED", [128, 512], F32)
            QTMP = tsb("QTMP", [128, HNT], F32)
            PTR3 = [tps(f"PTR3_{i}", [128, 128]) for i in range(4)]
            PQ = [tps(f"PQ{i}", [128, HNT]) for i in range(4)]
            NTTH = NHALF // 128

            for d in range(DC if UPTO >= 3 else 0):
                par = 0
                P.op("SY", lambda e, d=d: e.dma_start(
                    out=WQS[0][:], in_=wqt[d * 128:(d + 1) * 128, :]),
                    writes=[("WQS", 0)], is_dma=True, stream="WQS0")

                def wred(e, p=par, d=d):
                    for gg in range(4):
                        dst = WRED[:, gg * 128:(gg + 1) * 128]
                        sl = lambda h: WQS[p][:, (gg + 4 * h) * 128:(gg + 4 * h + 1) * 128]
                        e.tensor_tensor(dst, sl(0), sl(1), ADD)
                        e.drain()
                        e.tensor_tensor(dst, dst, sl(2), ADD)
                        e.drain()
                        e.tensor_tensor(dst, dst, sl(3), ADD)
                        e.drain()
                    e.tensor_copy(WQH[:, d * 512:(d + 1) * 512], WRED[:])
                    e.drain()
                    return e.tensor_tensor(WQL[:, d * 512:(d + 1) * 512], WRED[:],
                                           WQH[:, d * 512:(d + 1) * 512], SUB)
                P.op("DV", wred, reads=[("WQS", 0)], writes=[("WQHL", d), "WRED"])
                if d % 4 == 3:
                    emit_prepass(12)

            for half in range(2 if UPTO >= 3 else 0):
                for tt in range(NTTH):
                    it = half * NTTH + tt
                    P.op("SY", lambda e, r0=it * 128: e.dma_start(
                        out=XT3[0][:], in_=x_own[r0:r0 + 128, :]),
                        writes=[("XT3", 0)], is_dma=True, stream="XT30")
                    for d in range(DC):
                        k = (it * DC + d) % 4
                        P.op("PE", lambda e, d=d, k=k: e.transpose(
                            PTR3[k][:], XT3[0][:, d * 128:(d + 1) * 128], IDF[:]),
                            reads=[("XT3", 0), "IDF"], writes=[("PTR3", k)])

                        def spl3(e, d=d, tt=tt, k=k):
                            slot = PTR3[k][:]
                            dh = XSH[:, d * NHALF + tt * 128: d * NHALF + tt * 128 + 128]
                            dl = XSL[:, d * NHALF + tt * 128: d * NHALF + tt * 128 + 128]
                            e.tensor_copy(dh, slot)
                            e.drain()
                            return e.tensor_tensor(dl, slot, dh, SUB)
                        P.op("DV", spl3, reads=[("PTR3", k)], writes=[("XS", tt, d)])
                emit_prepass(24)

                def qmm(e):
                    for d in range(DC):
                        first, last = d == 0, d == DC - 1
                        for oc in range(OC):
                            wh = WQH[:, d * 512 + oc * 128: d * 512 + (oc + 1) * 128]
                            wl = WQL[:, d * 512 + oc * 128: d * 512 + (oc + 1) * 128]
                            rh = XSH[:, d * NHALF: d * NHALF + HNT]
                            rl = XSL[:, d * NHALF: d * NHALF + HNT]
                            e.matmul(PQ[oc][:], wh, rh, start=first, stop=False)
                            e.matmul(PQ[oc][:], wl, rh, start=False, stop=False)
                            instr = e.matmul(PQ[oc][:], wh, rl, start=False, stop=last)
                    return instr
                P.op("PE", qmm,
                     reads=[("WQHL", d) for d in range(DC)]
                           + [("XS", tt, d) for tt in range(NTTH) for d in range(DC)],
                     writes=[("PQ", oc) for oc in range(OC)])
                for oc in range(OC):
                    P.op("AC", lambda e, oc=oc: e.activation(
                        QTMP[:], PQ[oc][:], AF.Identity, bias=BQS[:, oc:oc + 1], scale=1.0),
                        reads=[("PQ", oc), "BQS"], writes=["QTMP"])

                    def qspl(e, oc=oc, half=half):
                        o0 = oc * NTOK + half * NHALF
                        e.tensor_copy(QH[:, o0:o0 + NHALF], QTMP[:])
                        e.drain()
                        return e.tensor_tensor(QL[:, o0:o0 + NHALF], QTMP[:],
                                               QH[:, o0:o0 + NHALF], SUB)
                    P.op("DV", qspl, reads=["QTMP"], writes=[("Q", half, oc)])
            P.barrier()

        with contextlib.ExitStack() as b4:
            tsb = lambda n, s, d: b4.enter_context(nc.sbuf_tensor(n, s, d))
            tps = lambda n, s: b4.enter_context(nc.psum_tensor(n, s, F32))
            tpsb = lambda n, s: b4.enter_context(nc.psum_tensor(n, s, BF16))
            MSK = tsb("MSK", [128, NQ * 512], BF16)
            for q in range(NQ):
                ld(MSK[:, q * 512:(q + 1) * 512], masks[q], [("MSK", q)], stream="MSKS")
            S32 = tsb("S32", [128, T], F32)
            P32 = tsb("P32", [128, T], F32)
            PHh = tsb("PHh", [128, T], BF16)
            PLl = tsb("PLl", [128, T], BF16)
            RS = tsb("RS", [128, 4], F32)
            MXT = tsb("MXT", [128, 4], F32)
            NEGM = tsb("NEGM", [128, 1], F32)
            RSUM = tsb("RSUM", [128, 1], F32)
            RR = tsb("RR", [128, 1], F32)
            PHT = [tsb(f"PHT{i}", [128, 128], BF16) for i in range(2)]
            PLT = [tsb(f"PLT{i}", [128, 128], BF16) for i in range(2)]
            PSC = [tps(f"PSC{i}", [128, 512]) for i in range(2)]
            ZP = tps("ZP", [128, 512])
            PTP = [tpsb(f"PTP{i}", [128, 256]) for i in range(2)]

            for q in range(NQ if UPTO >= 4 else 0):
                ext = cfg.exts[q]
                ntile = ext // 512
                nsub = ext // 128
                half = q // (NQ // 2)
                for ti in range(ntile):
                    def smm(e, q=q, ti=ti):
                        for oc in range(OC):
                            lh = QH[:, oc * NTOK + q * 128: oc * NTOK + (q + 1) * 128]
                            ll = QL[:, oc * NTOK + q * 128: oc * NTOK + (q + 1) * 128]
                            rh = KH[:, oc * T + ti * 512: oc * T + ti * 512 + 512]
                            rl = KL[:, oc * T + ti * 512: oc * T + ti * 512 + 512]
                            e.matmul(PSC[ti % 2][:], lh, rh, start=oc == 0, stop=False)
                            e.matmul(PSC[ti % 2][:], ll, rh, start=False, stop=False)
                            instr = e.matmul(PSC[ti % 2][:], lh, rl, start=False,
                                             stop=oc == OC - 1)
                        return instr
                    P.op("PE", smm,
                         reads=[("Q", half, oc) for oc in range(OC)]
                               + [("KHL", ti, oc) for oc in range(OC)],
                         writes=[("PSC", ti % 2)])

                    def sevac(e, q=q, ti=ti, last=(ti == ntile - 1)):
                        dst = S32[:, ti * 512:(ti + 1) * 512]
                        e.tensor_copy(dst, PSC[ti % 2][:])
                        e.drain()
                        if last:
                            e.tensor_tensor(dst, dst, MSK[:, q * 512:(q + 1) * 512], ADD)
                            e.drain()
                        return e.tensor_reduce(MXT[:, ti:ti + 1], dst, AX.X, MAXOP)
                    P.op("DV", sevac, reads=[("PSC", ti % 2), ("MSK", q)],
                         writes=[("S32", ti), ("MXT", ti)])

                def stats(e, ntile=ntile):
                    if ntile == 1:
                        return e.tensor_single_scalar(NEGM[:], MXT[:, 0:1], -1.0, MUL)
                    e.tensor_tensor(NEGM[:], MXT[:, 0:1], MXT[:, 1:2], MAXOP)
                    e.drain()
                    for ti in range(2, ntile):
                        e.tensor_tensor(NEGM[:], NEGM[:], MXT[:, ti:ti + 1], MAXOP)
                        e.drain()
                    return e.tensor_single_scalar(NEGM[:], NEGM[:], -1.0, MUL)
                P.op("DV", stats, reads=[("MXT", ti) for ti in range(ntile)], writes=["NEGM"])

                for ti in range(ntile):
                    P.op("AC", lambda e, ti=ti: e.activation(
                        P32[:, ti * 512:(ti + 1) * 512], S32[:, ti * 512:(ti + 1) * 512],
                        AF.Exp, bias=NEGM[:], scale=1.0, accum_out=RS[:, ti:ti + 1]),
                        reads=[("S32", ti), "NEGM"], writes=[("P32", ti), ("RS", ti)])

                def rsum(e, ntile=ntile):
                    if ntile == 1:
                        e.tensor_copy(RSUM[:], RS[:, 0:1])
                        e.drain()
                    else:
                        e.tensor_tensor(RSUM[:], RS[:, 0:1], RS[:, 1:2], ADD)
                        e.drain()
                        for ti in range(2, ntile):
                            e.tensor_tensor(RSUM[:], RSUM[:], RS[:, ti:ti + 1], ADD)
                            e.drain()
                    e.reciprocal(RR[:], RSUM[:])
                    e.drain()
                    return e.tensor_single_scalar(RR[:], RR[:], INV_SQRT_HS, MUL)
                P.op("DV", rsum, reads=[("RS", ti) for ti in range(ntile)],
                     writes=["RR", "RSUM"])

                for ti in range(ntile):
                    def pscale(e, ti=ti):
                        sl = P32[:, ti * 512:(ti + 1) * 512]
                        e.tensor_scalar(sl, sl, RR[:], None, MUL)
                        e.drain()
                        e.tensor_copy(PHh[:, ti * 512:(ti + 1) * 512], sl)
                        e.drain()
                        return e.tensor_tensor(PLl[:, ti * 512:(ti + 1) * 512], sl,
                                               PHh[:, ti * 512:(ti + 1) * 512], SUB)
                    P.op("DV", pscale, reads=[("P32", ti), "RR"], writes=[("PHL", ti)])

                for ss in range(nsub):
                    def ptt(e, ss=ss):
                        e.transpose(PTP[ss % 2][:, 0:128],
                                    PHh[:, ss * 128:(ss + 1) * 128], IDB[:])
                        return e.transpose(PTP[ss % 2][:, 128:256],
                                           PLl[:, ss * 128:(ss + 1) * 128], IDB[:])
                    P.op("PE", ptt, reads=[("PHL", ss // 4), "IDB"], writes=[("PTP", ss % 2)])

                    def ptev(e, ss=ss):
                        e.tensor_copy(PHT[ss % 2][:], PTP[ss % 2][:, 0:128])
                        e.drain()
                        return e.tensor_copy(PLT[ss % 2][:], PTP[ss % 2][:, 128:256])
                    P.op("DV", ptev, reads=[("PTP", ss % 2)], writes=[("PHT", ss % 2)])

                    def pv(e, ss=ss, nsub=nsub):
                        for oc in range(OC):
                            vh = VTH[:, ss * 512 + oc * 128: ss * 512 + (oc + 1) * 128]
                            vl = VTL[:, ss * 512 + oc * 128: ss * 512 + (oc + 1) * 128]
                            fo = oc * 128
                            e.matmul(ZP[:, fo:fo + 128], vh, PHT[ss % 2][:],
                                     start=(ss == 0 and oc == 0), stop=False)
                            e.matmul(ZP[:, fo:fo + 128], vh, PLT[ss % 2][:],
                                     start=False, stop=False)
                            instr = e.matmul(ZP[:, fo:fo + 128], vl, PHT[ss % 2][:],
                                             start=False, stop=(ss == nsub - 1 and oc == OC - 1))
                        return instr
                    P.op("PE", pv,
                         reads=[("PHT", ss % 2)] + [("VT", ss, oc) for oc in range(OC)],
                         writes=["ZPa"])

                def zspl(e, q=q):
                    for oc in range(OC):
                        o0 = oc * NTOK + q * 128
                        e.tensor_copy(ZDH[:, o0:o0 + 128], ZP[:, oc * 128:(oc + 1) * 128])
                        e.drain()
                        e.tensor_tensor(ZDL[:, o0:o0 + 128], ZP[:, oc * 128:(oc + 1) * 128],
                                        ZDH[:, o0:o0 + 128], SUB)
                        e.drain()
                    return e.nop()
                P.op("DV", zspl, reads=["ZPa"], writes=[("ZD", q)])
                emit_prepass(42)
            emit_prepass(NSL)
            assert _pp[0] == NSL, f"prepass incomplete: {_pp[0]}/{NSL}"
            def _ppdrain(e):
                e.drain()
                return e.nop()
            P.op("AC", _ppdrain, reads=list(_last_scr.values()))
            P.barrier()

    SX = sb("SX", [128, DC * NTOK], F32)

    with contextlib.ExitStack() as b5:
        tsb = lambda n, s, d: b5.enter_context(nc.sbuf_tensor(n, s, d))
        tps = lambda n, s: b5.enter_context(nc.psum_tensor(n, s, F32))
        XT5 = [tsb(f"XT5{i}", [128, D], F32) for i in range(2)]
        PSF = tsb("PSF", [128, D], F32)
        PSH = tsb("PSH", [128, OC * D], BF16)
        PSL = tsb("PSL", [128, OC * D], BF16)
        PRJ = [tsb(f"PRJ{i}", [128, D], F32) for i in range(2)]
        PTR5 = [tps(f"PTR5_{i}", [128, 128]) for i in range(4)]
        PO = [tps(f"PO{i}", [128, HNT]) for i in range(2)]

        for it in range(NTOK // 128 if UPTO >= 5 else 0):
            half = it // (NHALF // 128)
            P.op("SY", lambda e, r0=it * 128, p=it % 2: e.dma_start(
                out=XT5[p][:], in_=x_own[r0:r0 + 128, :]),
                writes=[("XT5", it % 2)], is_dma=True, stream=f"XT5{it % 2}")
            for d in range(DC):
                k = (it * DC + d) % 4
                P.op("PE", lambda e, p=it % 2, d=d, k=k: e.transpose(
                    PTR5[k][:], XT5[p][:, d * 128:(d + 1) * 128], IDF[:]),
                    reads=[("XT5", it % 2), "IDF"], writes=[("PTR5", k)])
                P.op("DV", lambda e, d=d, it=it, k=k: e.tensor_copy(
                    SX[:, d * NTOK + it * 128: d * NTOK + it * 128 + 128],
                    PTR5[k][:]),
                    reads=[("PTR5", k)], writes=[("SX", d, half)])

        for gg in range(4 if UPTO >= 5 else 0):
            for r in range(4):
                i = gg * 4 + r
                P.op("SY", lambda e, row=(gg + 4 * r) * 128, p=i % 2: e.dma_start(
                    out=PRJ[p][:], in_=proj[row:row + 128, :]),
                    writes=[("PRJ", i % 2)], is_dma=True, stream=f"PRJ{i % 2}")
                if r == 0:
                    P.op("DV", lambda e, p=i % 2: e.tensor_copy(PSF[:], PRJ[p][:]),
                         reads=[("PRJ", i % 2)], writes=["PSF"])
                else:
                    P.op("DV", lambda e, p=i % 2: e.tensor_tensor(PSF[:], PSF[:], PRJ[p][:], ADD),
                         reads=[("PRJ", i % 2)], writes=["PSF"])

            def pspl(e, gg=gg):
                e.tensor_copy(PSH[:, gg * D:(gg + 1) * D], PSF[:])
                e.drain()
                return e.tensor_tensor(PSL[:, gg * D:(gg + 1) * D], PSF[:],
                                       PSH[:, gg * D:(gg + 1) * D], SUB)
            P.op("DV", pspl, reads=["PSF"], writes=[("PS", gg)])

        for half in range(2 if UPTO >= 5 else 0):
            for d in range(DC):
                def pomm(e, half=half, d=d):
                    for gg in range(4):
                        lh = PSH[:, gg * D + d * 128: gg * D + (d + 1) * 128]
                        llw = PSL[:, gg * D + d * 128: gg * D + (d + 1) * 128]
                        rh = ZDH[:, gg * NTOK + half * NHALF: gg * NTOK + half * NHALF + HNT]
                        rl = ZDL[:, gg * NTOK + half * NHALF: gg * NTOK + half * NHALF + HNT]
                        e.matmul(PO[d % 2][:], lh, rh, start=gg == 0, stop=False)
                        e.matmul(PO[d % 2][:], lh, rl, start=False, stop=False)
                        instr = e.matmul(PO[d % 2][:], llw, rh, start=False, stop=gg == 3)
                    return instr
                P.op("PE", pomm,
                     reads=[("PS", gg) for gg in range(4)]
                           + [("ZD", qq) for qq in range(half * (NQ // 2), (half + 1) * (NQ // 2))],
                     writes=[("PO", d % 2)])
                sl0 = d * NTOK + half * NHALF
                P.op("DV", lambda e, d=d, sl0=sl0: e.tensor_tensor(
                    SX[:, sl0:sl0 + NHALF], PO[d % 2][:], SX[:, sl0:sl0 + NHALF], ADD),
                    reads=[("PO", d % 2)], writes=[("SX", d, half)])
        P.barrier()

    def layernorm(tag):
        with contextlib.ExitStack() as lns:
            tsb = lambda n, s, d: lns.enter_context(nc.sbuf_tensor(n, s, d))
            tps = lambda n, s: lns.enter_context(nc.psum_tensor(n, s, F32))
            SQT = [tsb(f"SQT{tag}{i}", [128, NTOK], F32) for i in range(2)]
            NM = tsb(f"NM{tag}", [1, NTOK], F32)
            M2 = tsb(f"M2{tag}", [1, NTOK], F32)
            VAR = tsb(f"VAR{tag}", [1, NTOK], F32)
            SRT = tsb(f"SRT{tag}", [1, NTOK], F32)
            R0 = tsb(f"R0{tag}", [1, NTOK], F32)
            T1 = tsb(f"T1{tag}", [1, NTOK], F32)
            RSTD = tsb(f"RSTD{tag}", [1, NTOK], F32)
            PS1 = tps(f"PS1{tag}", [1, NTOK])
            PS2 = tps(f"PS2{tag}", [1, NTOK])
            PBM = tps(f"PBM{tag}", [128, NTOK])
            PBR = tps(f"PBR{tag}", [128, NTOK])
            NNT = NTOK // HNT
            sxk = [("SX", d, h) for d in range(DC) for h in range(2)]

            def s1mm(e):
                for d in range(DC):
                    for nt in range(NNT):
                        instr = e.matmul(PS1[:, nt * HNT:(nt + 1) * HNT], ONEC[:],
                                         SX[:, d * NTOK + nt * HNT: d * NTOK + nt * HNT + HNT],
                                         start=(d == 0 and (nt * HNT) % 512 == 0),
                                         stop=(d == DC - 1 and (((nt + 1) * HNT) % 512 == 0
                                                                or nt == NNT - 1)))
                return instr
            P.op("PE", s1mm, reads=["ONEC"] + sxk, writes=["PS1" + tag])
            for d in range(DC):
                P.op("AC", lambda e, d=d: e.activation(
                    SQT[d % 2][:], SX[:, d * NTOK:(d + 1) * NTOK], AF.Square),
                    reads=[("SX", d, 0), ("SX", d, 1)], writes=[("SQT" + tag, d % 2)])

                def s2mm(e, d=d):
                    for nt in range(NNT):
                        instr = e.matmul(PS2[:, nt * HNT:(nt + 1) * HNT], ONEC[:],
                                         SQT[d % 2][:, nt * HNT: nt * HNT + HNT],
                                         start=(d == 0 and (nt * HNT) % 512 == 0),
                                         stop=(d == DC - 1 and (((nt + 1) * HNT) % 512 == 0
                                                                or nt == NNT - 1)))
                    return instr
                P.op("PE", s2mm, reads=[("SQT" + tag, d % 2), "ONEC"], writes=["PS2" + tag])

            def small(e):
                e.tensor_single_scalar(NM[:], PS1[:], -1.0 / D, MUL)
                e.drain()
                e.tensor_tensor(M2[:], NM[:], NM[:], MUL)
                e.drain()
                e.scalar_tensor_tensor(VAR[:], PS2[:], 1.0 / D, M2[:], MUL, SUB)
                e.drain()
                return e.tensor_single_scalar(VAR[:], VAR[:], 1e-5, ADD)
            P.op("DV", small, reads=["PS1" + tag, "PS2" + tag],
                 writes=["NM" + tag, "VAR" + tag])
            P.op("AC", lambda e: e.activation(SRT[:], VAR[:], AF.Sqrt),
                 reads=["VAR" + tag], writes=["SRT" + tag])

            def newt(e):
                e.reciprocal(R0[:], SRT[:])
                e.drain()
                e.tensor_tensor(T1[:], R0[:], R0[:], MUL)
                e.drain()
                e.tensor_tensor(T1[:], T1[:], VAR[:], MUL)
                e.drain()
                e.tensor_scalar(T1[:], T1[:], -0.5, 1.5, MUL, ADD)
                e.drain()
                return e.tensor_tensor(RSTD[:], R0[:], T1[:], MUL)
            P.op("DV", newt, reads=["SRT" + tag, "VAR" + tag], writes=["RSTD" + tag])

            def bmm(e):
                for nt in range(NNT):
                    st_ = (nt * HNT) % 512 == 0
                    sp_ = ((nt + 1) * HNT) % 512 == 0 or nt == NNT - 1
                    e.matmul(PBM[:, nt * HNT:(nt + 1) * HNT], ONER[:],
                             NM[:, nt * HNT: nt * HNT + HNT], start=st_, stop=sp_)
                    instr = e.matmul(PBR[:, nt * HNT:(nt + 1) * HNT], ONER[:],
                                     RSTD[:, nt * HNT: nt * HNT + HNT], start=st_, stop=sp_)
                return instr
            P.op("PE", bmm, reads=["NM" + tag, "RSTD" + tag, "ONER"],
                 writes=["PBM" + tag, "PBR" + tag])

            for d in range(DC):
                def apply(e, d=d):
                    for h in range(2):
                        sl = SX[:, d * NTOK + h * NHALF: d * NTOK + h * NHALF + NHALF]
                        e.tensor_tensor(sl, sl, PBM[:, h * NHALF:(h + 1) * NHALF], ADD)
                        e.drain()
                        e.tensor_tensor(sl, sl, PBR[:, h * NHALF:(h + 1) * NHALF], MUL)
                        e.drain()
                    return e.nop()
                P.op("DV", apply, reads=["PBM" + tag, "PBR" + tag],
                     writes=[("SX", d, 0), ("SX", d, 1)])
            P.barrier()

    if UPTO >= 6:
        layernorm("L1")

    with contextlib.ExitStack() as b7:
        tsb = lambda n, s, d: b7.enter_context(nc.sbuf_tensor(n, s, d))
        tps = lambda n, s: b7.enter_context(nc.psum_tensor(n, s, F32))
        XSH7 = tsb("XSH7", [128, DC * NHALF], BF16)
        XSL7 = tsb("XSL7", [128, DC * NHALF], BF16)
        QG = [tsb(f"QG{i}", [128, DC * G], BF16) for i in range(2)]
        QU = [tsb(f"QU{i}", [128, DC * G], BF16) for i in range(2)]
        QD = tsb("QD", [128, GKC * D], BF16)
        G1 = tsb("G1", [128, HNT], F32)
        G1S = tsb("G1S", [128, HNT], F32)
        U1 = tsb("U1", [128, HNT], F32)
        FT = tsb("FT", [128, HNT], F32)
        FH = tsb("FH", [128, GKC * HNT], BF16)
        FL = tsb("FL", [128, GKC * HNT], BF16)
        PG = [tps(f"PG{i}", [128, HNT]) for i in range(2)]
        PU = [tps(f"PU{i}", [128, HNT]) for i in range(2)]
        PD = [tps(f"PD{i}", [128, HNT]) for i in range(2)]

        for half in range(2 if UPTO >= 7 else 0):
            for d in range(DC):
                def xspl(e, d=d, half=half):
                    src = SX[:, d * NTOK + half * NHALF: d * NTOK + half * NHALF + NHALF]
                    e.tensor_copy(XSH7[:, d * NHALF:(d + 1) * NHALF], src)
                    e.drain()
                    return e.tensor_tensor(XSL7[:, d * NHALF:(d + 1) * NHALF], src,
                                           XSH7[:, d * NHALF:(d + 1) * NHALF], SUB)
                P.op("DV", xspl, reads=[("SX", d, half)], writes=[("XS7", d)])

            for grp in range(NGRP):
                par = grp % 2
                if half == 0:
                    ensure_prepass(GRP_BOUND[grp])
                for kc in range(DC):
                    P.op("SY", lambda e, kc=kc, grp=grp, p=par: e.dma_start(
                        out=QG[p][:, kc * G:(kc + 1) * G],
                        in_=sc_g[kc][:, grp * G:(grp + 1) * G]),
                        reads=[("scr", "g", kc, w_) for w_ in
                               range(grp * G // W, min(I // W, ((grp + 1) * G + W - 1) // W))],
                        writes=[("QG", par, kc)], is_dma=True, stream=f"QG{par}")
                    P.op("SY", lambda e, kc=kc, grp=grp, p=par: e.dma_start(
                        out=QU[p][:, kc * G:(kc + 1) * G],
                        in_=sc_u[kc][:, grp * G:(grp + 1) * G]),
                        reads=[("scr", "u", kc, w_) for w_ in
                               range(grp * G // W, min(I // W, ((grp + 1) * G + W - 1) // W))],
                        writes=[("QU", par, kc)], is_dma=True, stream=f"QU{par}")
                for kc in range(GKC):
                    icg = grp * GKC + kc
                    P.op("SY", lambda e, icg=icg, kc=kc: e.dma_start(
                        out=QD[:, kc * D:(kc + 1) * D], in_=sc_d[icg][:, :]),
                        reads=[("scr", "d", icg, w_) for w_ in range(D // W)],
                        writes=[("QD", kc)], is_dma=True, stream="QD")

                for kc in range(GKC):
                    icg = grp * GKC + kc
                    for MAT, PM, QW, SCL, outf, outk in (
                            ("G", PG, QG, GSC, G1, "G1"), ("U", PU, QU, USC, U1, "U1")):
                        def gmm(e, kc=kc, p=par, PM=PM, QW=QW):
                            for d in range(DC):
                                lw = QW[p][:, d * G + kc * 128: d * G + kc * 128 + 128]
                                e.matmul(PM[kc % 2][:], lw, XSH7[:, d * NHALF: d * NHALF + HNT],
                                         start=d == 0, stop=False)
                                instr = e.matmul(PM[kc % 2][:], lw,
                                                 XSL7[:, d * NHALF: d * NHALF + HNT],
                                                 start=False, stop=d == DC - 1)
                            return instr
                        P.op("PE", gmm,
                             reads=[("Q" + MAT, par, d) for d in range(DC)]
                                   + [("XS7", d) for d in range(DC)],
                             writes=[("P" + MAT, kc % 2)])
                        P.op("DV", lambda e, kc=kc, icg=icg, PM=PM, SCL=SCL, outf=outf:
                             e.tensor_scalar(outf[:], PM[kc % 2][:], SCL[:, icg:icg + 1],
                                             None, MUL),
                             reads=[("P" + MAT, kc % 2),
                                    "GSC" if MAT == "G" else "USC"],
                             writes=[outk])
                    P.op("AC", lambda e: e.activation(G1S[:], G1[:], AF.Silu),
                         reads=["G1"], writes=["G1S"])

                    def fuse(e, kc=kc):
                        e.tensor_tensor(FT[:], G1S[:], U1[:], MUL)
                        e.drain()
                        e.tensor_copy(FH[:, kc * HNT:(kc + 1) * HNT], FT[:])
                        e.drain()
                        return e.tensor_tensor(FL[:, kc * HNT:(kc + 1) * HNT], FT[:],
                                               FH[:, kc * HNT:(kc + 1) * HNT], SUB)
                    P.op("DV", fuse, reads=["G1S", "U1"], writes=[("F", kc), "FT"])

                for h in range(DC):
                    def dmm(e, h=h):
                        for kc in range(GKC):
                            lw = QD[:, kc * D + h * 128: kc * D + (h + 1) * 128]
                            e.matmul(PD[h % 2][:], lw, FH[:, kc * HNT:(kc + 1) * HNT],
                                     start=kc == 0, stop=False)
                            instr = e.matmul(PD[h % 2][:], lw, FL[:, kc * HNT:(kc + 1) * HNT],
                                             start=False, stop=kc == GKC - 1)
                        return instr
                    P.op("PE", dmm,
                         reads=[("QD", kc) for kc in range(GKC)]
                               + [("F", kc) for kc in range(GKC)],
                         writes=[("PD", h % 2)])
                    sl0 = h * NTOK + half * NHALF
                    P.op("DV", lambda e, h=h, sl0=sl0: e.scalar_tensor_tensor(
                        SX[:, sl0:sl0 + NHALF], PD[h % 2][:], DSC[:, h:h + 1],
                        SX[:, sl0:sl0 + NHALF], MUL, ADD),
                        reads=[("PD", h % 2), "DSC"], writes=[("SX", h, half)])
        P.barrier()



    if UPTO >= 8:
        layernorm("L2")

    with contextlib.ExitStack() as b8:
        tsb = lambda n, s, d: b8.enter_context(nc.sbuf_tensor(n, s, d))
        tps = lambda n, s: b8.enter_context(nc.psum_tensor(n, s, F32))
        OUTT = [tsb(f"OUTT{i}", [128, D], F32) for i in range(2)]
        PTR8 = [tps(f"PTR8_{i}", [128, 128]) for i in range(4)]
        for it in range(NTOK // 128 if UPTO >= 9 else 0):
            half = it // (NHALF // 128)
            for d in range(DC):
                k = (it * DC + d) % 4
                P.op("PE", lambda e, it=it, d=d, k=k: e.transpose(
                    PTR8[k][:],
                    SX[:, d * NTOK + it * 128: d * NTOK + it * 128 + 128], IDF[:]),
                    reads=[("SX", d, half), "IDF"], writes=[("PTR8", k)])
                P.op("DV", lambda e, it=it, d=d, k=k: e.tensor_copy(
                    OUTT[it % 2][:, d * 128:(d + 1) * 128], PTR8[k][:]),
                    reads=[("PTR8", k)], writes=[("OUT", it % 2, d)])
            P.op("GP", lambda e, it=it: e.dma_start(
                out=y[it * 128:(it + 1) * 128, :], in_=OUTT[it % 2][:]),
                reads=[("OUT", it % 2, d) for d in range(DC)],
                writes=[("ydone", it)], is_dma=True, stream="YOUT")
        if UPTO >= 9:
            def _youtw(e):
                e.drain()
                return e.nop()
            P.op("GP", _youtw, reads=[("ydone", it) for it in range(NTOK // 128)])

    P.lower(nc, es)
    with nc.Block() as blk:
        @blk.tensor
        def _(eng):
            P.emit_engine("PE", eng)

        @blk.vector
        def _(eng):
            P.emit_engine("DV", eng)

        @blk.scalar
        def _(eng):
            P.emit_engine("AC", eng)

        @blk.gpsimd
        def _(eng):
            P.emit_engine("GP", eng)

        @blk.sync
        def _(eng):
            P.emit_engine("SY", eng)

    es.close()
    return nc


def make_core_inputs(cfg, X, WQ, bQ, WK, bK, WV, bV, PRJ, GW, UW, DW, GS, US, DS):
    NQ = cfg.NQ
    wqt = np.ascontiguousarray(WQ.T)
    wkt = np.ascontiguousarray(WK.T)
    wvt = np.ascontiguousarray(WV.T)
    prj = np.ascontiguousarray(PRJ)
    gi = np.ascontiguousarray(GW).view(np.int32)
    ui = np.ascontiguousarray(UW).view(np.int32)
    di = np.ascontiguousarray(DW).view(np.int32)
    bq128 = np.ascontiguousarray(bQ.reshape(16, 128).T)
    bkv = np.ascontiguousarray(
        np.concatenate([bK.reshape(4, 128).T, bV.reshape(4, 128).T], axis=1))
    gsc = np.ascontiguousarray(GS.reshape(cfg.IC, 128).T)
    usc = np.ascontiguousarray(US.reshape(cfg.IC, 128).T)
    dsc = np.ascontiguousarray(DS.reshape(cfg.DC, 128).T)
    eyef = np.eye(128, dtype=np.float32)
    eyeb = np.eye(128, dtype=np.float32).astype(ml_dtypes.bfloat16)

    maps = []
    for core in range(cfg.ncores):
        b, m = core // 2, core % 2
        chunks = cfg.chunks[m]
        rows = np.concatenate([np.arange(j * 128, (j + 1) * 128) for j in chunks])
        mk = np.zeros((NQ, 128, 512), dtype=np.float32)
        for lq, j in enumerate(chunks):
            s0 = cfg.exts[lq] - 512
            col = s0 + np.arange(512)[None, :]
            row = j * 128 + np.arange(128)[:, None]
            mk[lq] = np.where(col <= row, 0.0, -30000.0)
        maps.append({
            "x_seq": np.ascontiguousarray(X[b]),
            "x_own": np.ascontiguousarray(X[b][rows]),
            "wqt": wqt, "wkt": wkt, "wvt": wvt, "proj": prj,
            "gate_i": gi, "up_i": ui, "down_i": di,
            "bq": bq128, "bkv": bkv, "gsc": gsc, "usc": usc, "dsc": dsc,
            "masks": mk.astype(ml_dtypes.bfloat16),
            "eyef": eyef, "eyeb": eyeb,
        })
    return maps


def assemble_output(cfg, results):
    Y = np.zeros((cfg.B, cfg.T, cfg.D), dtype=np.float32)
    for core, res in enumerate(results):
        b, m = core // 2, core % 2
        yc = np.asarray(res["y"])
        for lq, j in enumerate(cfg.chunks[m]):
            Y[b, j * 128:(j + 1) * 128] = yc[lq * 128:(lq + 1) * 128]
    return Y


def kernel(L0_attn_x, L0_attn_WQ, L0_attn_bQ, L0_attn_WK, L0_attn_bK,
           L0_attn_WV, L0_attn_bV, L0_attn_proj, L0_ffn_gate_w, L0_ffn_up_w,
           L0_ffn_down_w, L0_ffn_gate_scale, L0_ffn_up_scale, L0_ffn_down_scale):
    cfg = Cfg(B=4, T=2048, D=2048, I=6144, igroup=384, prew=512)
    X = np.asarray(L0_attn_x, dtype=np.float32)
    maps = make_core_inputs(
        cfg, X,
        np.asarray(L0_attn_WQ, np.float32), np.asarray(L0_attn_bQ, np.float32),
        np.asarray(L0_attn_WK, np.float32), np.asarray(L0_attn_bK, np.float32),
        np.asarray(L0_attn_WV, np.float32), np.asarray(L0_attn_bV, np.float32),
        np.asarray(L0_attn_proj, np.float32),
        np.asarray(L0_ffn_gate_w, np.float32), np.asarray(L0_ffn_up_w, np.float32),
        np.asarray(L0_ffn_down_w, np.float32),
        np.asarray(L0_ffn_gate_scale, np.float32), np.asarray(L0_ffn_up_scale, np.float32),
        np.asarray(L0_ffn_down_scale, np.float32))
    nc = build_program(cfg)
    res = run_bass_kernel_spmd(nc, maps, core_ids=list(range(cfg.ncores)))
    return assemble_output(cfg, res.results)


# version 08
# speedup vs baseline: 1.3545x; 1.0149x over previous
import contextlib

import numpy as np
import ml_dtypes

import concourse.bass as bass
import concourse.mybir as mybir
from concourse.bass_utils import run_bass_kernel_spmd
from concourse.alu_op_type import AluOpType

AF = mybir.ActivationFunctionType
AX = mybir.AxisListType
F32 = mybir.dt.float32
BF16 = mybir.dt.bfloat16
I32 = mybir.dt.int32

C0 = 12582912.0
CINV = 1572864.0
EXPBITS = 0x7F800000
EXPCLAMP = float(0x3C800000)
EXPINV = float(0x7F000000)
INV_SQRT_HS = 0.08838834764831845
ADD, SUB, MUL, MAXOP = AluOpType.add, AluOpType.subtract, AluOpType.mult, AluOpType.max


class Cfg:
    def __init__(self, B, T, D, I, igroup=384, prew=512, upto=9):
        self.B, self.T, self.D, self.I = B, T, D, I
        self.upto = upto
        self.OC = 4
        self.DC = D // 128
        self.IC = I // 128
        self.TQC = T // 128
        self.NQ = self.TQC // 2
        self.NTOK = self.NQ * 128
        self.NHALF = self.NTOK // 2
        self.ncores = 2 * B
        self.G = igroup
        self.GKC = igroup // 128
        self.NGRP = I // igroup
        self.PREW = min(prew, I, D)
        self.chunks = [[j for j in range(self.TQC) if (j // 2) % 2 == m] for m in (0, 1)]
        self.exts = [512 * ((j * 128 + 128 + 511) // 512) for j in self.chunks[0]]
        assert self.exts == [512 * ((j * 128 + 128 + 511) // 512) for j in self.chunks[1]]
        assert self.NHALF <= 512 and T % 512 == 0 and I % igroup == 0


ENGINES = ("PE", "DV", "AC", "GP", "SY")


class Op:
    __slots__ = ("eng", "emit", "idx", "deps", "is_dma", "sem_val", "need_inc", "stream")

    def __init__(self, eng, emit, is_dma, stream=None):
        self.eng, self.emit, self.is_dma = eng, emit, is_dma
        self.stream = stream
        self.idx = -1
        self.deps = []
        self.sem_val = 0
        self.need_inc = is_dma


class Prog:
    def __init__(self):
        self.ops = {e: [] for e in ENGINES}
        self.w = {}
        self.r = {}
        self.scnt = {}

    def op(self, eng, emit, reads=(), writes=(), is_dma=False, after=(), stream=None):
        assert (stream is not None) == is_dma
        o = Op(eng, emit, is_dma, stream)
        o.idx = len(self.ops[eng])
        self.ops[eng].append(o)
        if is_dma:
            self.scnt[stream] = self.scnt.get(stream, 0) + 1
            o.sem_val = 16 * self.scnt[stream]
        deps = list(after)
        for k in reads:
            lw = self.w.get(k)
            if lw is not None:
                deps.append(lw)
            self.r.setdefault(k, []).append(o)
        for k in writes:
            lw = self.w.get(k)
            if lw is not None:
                deps.append(lw)
            deps.extend(self.r.get(k, ()))
            self.r[k] = []
            self.w[k] = o
        best = {}
        dtgt = {}
        for d in deps:
            if d is o:
                continue
            if d.is_dma:
                key = ("D", d.stream)
                tgt = 16 * self.scnt[d.stream] - (16 if (is_dma and stream == d.stream) else 0)
                if key not in dtgt or tgt > dtgt[key][1]:
                    dtgt[key] = (d, tgt)
            else:
                cur = best.get(d.eng)
                if cur is None or d.idx > cur.idx:
                    best[d.eng] = d
        o.deps = list(best.values()) + list(dtgt.values())
        return o

    def barrier(self):
        lasts = [self.ops[e][-1] for e in ENGINES if self.ops[e]]
        def _fence_emit(eng):
            eng.drain()
            return eng.nop()
        for e in ENGINES:
            self.op(e, _fence_emit, after=lasts)

    def lower(self, nc, es):
        for e in ENGINES:
            for o in self.ops[e]:
                for d in o.deps:
                    if isinstance(d, tuple):
                        continue
                    if d.eng != o.eng and not d.is_dma:
                        d.need_inc = True
        self.sems = {}
        self.dsems = {}
        for e in ENGINES:
            self.sems[e] = es.enter_context(nc.semaphore(f"sem_{e}"))
            cnt = 0
            for o in self.ops[e]:
                if o.is_dma:
                    if o.stream not in self.dsems:
                        self.dsems[o.stream] = es.enter_context(nc.semaphore(f"dma_{o.stream}"))
                elif o.need_inc:
                    cnt += 1
                    o.sem_val = cnt
                else:
                    o.sem_val = cnt

    def emit_engine(self, ename, eng):
        waited = {}
        last_drain = -1
        for o in self.ops[ename]:
            need_drain = False
            for d in o.deps:
                if isinstance(d, tuple):
                    dd, tgt = d
                    key = ("D", dd.stream)
                    if tgt > waited.get(key, 0):
                        eng.wait_ge(self.dsems[dd.stream], tgt)
                        waited[key] = tgt
                elif d.eng == ename:
                    if d.idx > last_drain:
                        need_drain = True
                else:
                    if d.sem_val > waited.get(d.eng, 0):
                        eng.wait_ge(self.sems[d.eng], d.sem_val)
                        waited[d.eng] = d.sem_val
            if need_drain and ename in ("DV", "AC", "GP"):
                eng.drain()
                last_drain = o.idx - 1
            instr = o.emit(eng)
            if o.is_dma:
                instr.then_inc(self.dsems[o.stream], 16)
            elif o.need_inc:
                instr.then_inc(self.sems[ename], 1)


def build_program(cfg: Cfg):
    nc = bass.Bass("TRN2", target_bir_lowering=False)
    T, D, I = cfg.T, cfg.D, cfg.I
    DC, IC, OC, NQ = cfg.DC, cfg.IC, cfg.OC, cfg.NQ
    NTOK, NHALF = cfg.NTOK, cfg.NHALF
    G, GKC, NGRP = cfg.G, cfg.GKC, cfg.NGRP
    HNT = NHALF

    dram = lambda n, s, d, k="ExternalInput": nc.dram_tensor(n, s, d, kind=k).ap()
    x_seq_t = dram("x_seq_t", [D, T], F32)
    x_own_t = dram("x_own_t", [D, NTOK], F32)
    wqt = dram("wqt", [D, D], F32)
    wkt = dram("wkt", [D, 512], F32)
    wvt = dram("wvt", [D, 512], F32)
    proj = dram("proj", [D, D], F32)
    gate_i = dram("gate_i", [D, I], I32)
    up_i = dram("up_i", [D, I], I32)
    down_i = dram("down_i", [I, D], I32)
    bq = dram("bq", [128, 16], F32)
    bkv = dram("bkv", [128, 8], F32)
    bvr = dram("bvr", [1, 512], F32)
    gsc = dram("gsc", [128, IC], F32)
    usc = dram("usc", [128, IC], F32)
    dsc = dram("dsc", [128, DC], F32)
    masks = dram("masks", [NQ, 128, 512], BF16)
    eyef = dram("eyef", [128, 128], F32)
    eyeb = dram("eyeb", [128, 128], BF16)
    y_t = dram("y_t", [D, NTOK], F32, k="ExternalOutput")
    sc_g = dram("sc_g", [DC, 128, I], BF16, k="Internal")
    sc_u = dram("sc_u", [DC, 128, I], BF16, k="Internal")
    sc_d = dram("sc_d", [IC, 128, D], BF16, k="Internal")

    es = contextlib.ExitStack()
    sb = lambda n, s, d: es.enter_context(nc.sbuf_tensor(n, s, d))
    P = Prog()

    IDF = sb("IDF", [128, 128], F32)
    IDB = sb("IDB", [128, 128], BF16)
    ONEC = sb("ONEC", [128, 1], F32)
    ONER = sb("ONER", [1, 128], F32)
    BQR = sb("BQR", [128, 16], F32)
    BQS = sb("BQS", [128, 4], F32)
    BKV = sb("BKV", [128, 8], F32)
    GSC = sb("GSC", [128, IC], F32)
    USC = sb("USC", [128, IC], F32)
    DSC = sb("DSC", [128, DC], F32)

    def ld(dst_ap, src_ap, keys, stream="CONST"):
        P.op("SY", lambda e, d=dst_ap, s=src_ap: e.dma_start(out=d, in_=s),
             writes=keys, is_dma=True, stream=stream)

    for dd, ss_, kk in ((IDF, eyef, "IDF"), (IDB, eyeb, "IDB"), (BQR, bq, "BQR"),
                        (BKV, bkv, "BKV"), (GSC, gsc, "GSC"), (USC, usc, "USC"),
                        (DSC, dsc, "DSC")):
        ld(dd[:], ss_, [kk])
    P.op("GP", lambda e: e.memset(ONEC[:], 1.0), writes=["ONEC"])
    P.op("GP", lambda e: e.memset(ONER[:], 1.0), writes=["ONER"])

    def bqs_emit(e):
        for gg in range(4):
            e.tensor_tensor(BQS[:, gg:gg + 1], BQR[:, gg:gg + 1], BQR[:, gg + 4:gg + 5], ADD)
            e.drain()
            for r in (8, 12):
                e.tensor_tensor(BQS[:, gg:gg + 1], BQS[:, gg:gg + 1],
                                BQR[:, gg + r:gg + r + 1], ADD)
                e.drain()
        return e.nop()
    P.op("DV", bqs_emit, reads=["BQR"], writes=["BQS"])

    W = cfg.PREW
    gate_sl = [("g", gate_i, sc_g, kc, ws) for ws in range(I // W) for kc in range(DC)]
    up_sl = [("u", up_i, sc_u, kc, ws) for ws in range(I // W) for kc in range(DC)]
    down_sl = [("d", down_i, sc_d, kc, ws) for kc in range(IC) for ws in range(D // W)]
    pslices = []
    ng, nu, nd = len(gate_sl), len(up_sl), len(down_sl)
    gi_, ui_, di_ = 0, 0, 0
    for k in range(ng + nu + nd):
        pick = k % 3
        if pick == 0 and gi_ < ng:
            pslices.append(gate_sl[gi_]); gi_ += 1
        elif pick == 1 and ui_ < nu:
            pslices.append(up_sl[ui_]); ui_ += 1
        elif di_ < nd:
            pslices.append(down_sl[di_]); di_ += 1
        elif gi_ < ng:
            pslices.append(gate_sl[gi_]); gi_ += 1
        elif ui_ < nu:
            pslices.append(up_sl[ui_]); ui_ += 1
    NSL = len(pslices)
    assert NSL == ng + nu + nd

    def grp_bound(grp):
        ws_need = ((grp + 1) * G - 1) // W
        icg_max = (grp + 1) * GKC - 1
        bound = 0
        for i, (tag, _, _, kc, ws) in enumerate(pslices):
            need = (tag in ("g", "u") and ws <= ws_need) or (tag == "d" and kc <= icg_max)
            if need:
                bound = i + 1
        return bound
    GRP_BOUND = [grp_bound(g_) for g_ in range(NGRP)]

    def ensure_prepass(bound):
        pass

    UPTO = cfg.upto
    ZDH = sb("ZDH", [128, OC * NTOK], BF16)
    ZDL = sb("ZDL", [128, OC * NTOK], BF16)

    with contextlib.ExitStack() as a_es:
        asb = lambda n, s, d: a_es.enter_context(nc.sbuf_tensor(n, s, d))
        KH = asb("KH", [128, OC * T], BF16)
        KL = asb("KL", [128, OC * T], BF16)
        VTH = asb("VTH", [128, (T // 128) * 512], BF16)
        VTL = asb("VTL", [128, (T // 128) * 512], BF16)
        QH = asb("QH", [128, OC * NTOK], BF16)
        QL = asb("QL", [128, OC * NTOK], BF16)
        psb = lambda n, s, d: a_es.enter_context(nc.sbuf_tensor(n, s, d))
        UB = [psb(f"UB{i}", [128, W], I32) for i in range(2)]
        SCB = [psb(f"SCB{i}", [128, W], I32) for i in range(2)]
        SCIB = [psb(f"SCIB{i}", [128, W], I32) for i in range(2)]
        YB = [psb(f"YB{i}", [128, W], F32) for i in range(2)]
        IIB = [psb(f"IIB{i}", [128, W], I32) for i in range(2)]
        Q1B = [psb(f"Q1B{i}", [128, W], F32) for i in range(2)]
        WQB = [psb(f"WQB{i}", [128, W], BF16) for i in range(2)]
        _pp = [0]
        _last_scr = {}

        def _prepass_front(i):
            tag, src_, dst, kc, ws = pslices[i]
            p = i % 2
            P.op("SY", lambda e, s=src_, kc=kc, ws=ws, p=p: e.dma_start(
                out=UB[p][:], in_=s[kc * 128:(kc + 1) * 128, ws * W:(ws + 1) * W]),
                writes=[("U", p)], is_dma=True, stream=f"U{p}")

            def dv1(e, p=p):
                e.tensor_single_scalar(SCB[p][:], UB[p][:], EXPBITS, AluOpType.bitwise_and)
                e.drain()
                e.tensor_single_scalar(SCB[p][:], SCB[p][:], EXPCLAMP, MAXOP)
                e.drain()
                return e.tensor_scalar(SCIB[p][:], SCB[p][:], -1.0, EXPINV, MUL, ADD)
            P.op("DV", dv1, reads=[("U", p)], writes=[("SC", p), ("SCI", p)])
            P.op("GP", lambda e, p=p: e.tensor_tensor(
                YB[p][:], UB[p][:].bitcast(F32), SCIB[p][:].bitcast(F32), MUL),
                reads=[("U", p), ("SCI", p)], writes=[("Y", p)])

            def ac1(e, p=p):
                e.activation(IIB[p][:], YB[p][:], AF.Copy, bias=C0, scale=8.0)
                e.drain()
                return e.activation(Q1B[p][:], IIB[p][:], AF.Copy, bias=-CINV, scale=0.125)
            P.op("AC", ac1, reads=[("Y", p)], writes=[("II", p), ("Q1", p)])

        def _prepass_back(i):
            tag, src_, dst, kc, ws = pslices[i]
            p = i % 2
            P.op("GP", lambda e, p=p: e.tensor_tensor(
                WQB[p][:], Q1B[p][:], SCB[p][:].bitcast(F32), MUL),
                reads=[("Q1", p), ("SC", p)], writes=[("WQ", p)])
            P.op("AC", lambda e, d=dst, kc=kc, ws=ws, p=p: e.dma_start(
                out=d[kc][:, ws * W:(ws + 1) * W], in_=WQB[p][:]),
                reads=[("WQ", p)], writes=[("scr", tag, kc, ws)], is_dma=True,
                stream=f"WQ{p}")
            _last_scr[p] = ("scr", tag, kc, ws)

        def emit_prepass(n):
            for _ in range(n):
                i = _pp[0]
                if i >= NSL:
                    return
                _pp[0] += 1
                _prepass_front(i)
                if i >= 1:
                    _prepass_back(i - 1)
                if i == NSL - 1:
                    _prepass_back(i)



        SEQ = 512
        NSC = T // SEQ
        NTT = SEQ // 128
        BVR = asb("BVR", [1, 512], F32)
        BVB = asb("BVB", [128, 512], F32)
        ld(BVR[:], bvr, ["BVR"], stream="BVRS")
        with contextlib.ExitStack() as b2:
            tsb = lambda n, s, d: b2.enter_context(nc.sbuf_tensor(n, s, d))
            tps = lambda n, s: b2.enter_context(nc.psum_tensor(n, s, F32))
            XF = [tsb(f"XF{i}", [128, SEQ], F32) for i in range(2)]
            SQH = tsb("SQH", [128, DC * SEQ], BF16)
            SQL = tsb("SQL", [128, DC * SEQ], BF16)
            WRB = [tsb(f"WRB{i}", [128, 512], F32) for i in range(2)]
            WSH = [tsb(f"WSH{i}", [128, 512], BF16) for i in range(2)]
            WSL = [tsb(f"WSL{i}", [128, 512], BF16) for i in range(2)]
            KTMP = tsb("KTMP", [128, 512], F32)
            VTMP = tsb("VTMP", [128, 512], F32)
            PK = [tps(f"PK{i}", [128, 512]) for i in range(4)]
            PV = [tps(f"PV{i}", [128, 512]) for i in range(4)]

            P.op("PE", lambda e: e.matmul(PK[0][:], ONER[:], BVR[:], start=True, stop=True),
                 reads=["ONER", "BVR"], writes=[("PK", 0)])
            P.op("DV", lambda e: e.tensor_copy(BVB[:], PK[0][:]),
                 reads=[("PK", 0)], writes=["BVB"])

            for sc in range(NSC):
                for d in range(DC):
                    ix = sc * DC + d
                    P.op("SY", lambda e, d=d, sc=sc, p=ix % 2: e.dma_start(
                        out=XF[p][:], in_=x_seq_t[d * 128:(d + 1) * 128,
                                                  sc * SEQ:(sc + 1) * SEQ]),
                        writes=[("XF", ix % 2)], is_dma=True, stream=f"XF{ix % 2}")

                    def spl(e, d=d, p=ix % 2):
                        dh = SQH[:, d * SEQ:(d + 1) * SEQ]
                        dl = SQL[:, d * SEQ:(d + 1) * SEQ]
                        e.tensor_copy(dh, XF[p][:])
                        e.drain()
                        return e.tensor_tensor(dl, XF[p][:], dh, SUB)
                    P.op("DV", spl, reads=[("XF", ix % 2)], writes=[("SQ", d)])
                emit_prepass(18)
                for d in range(DC):
                    par = d % 2
                    P.op("SY", lambda e, d=d, p=par: e.dma_start(
                        out=WRB[p][:], in_=wkt[d * 128:(d + 1) * 128, :]),
                        writes=[("WRB", par)], is_dma=True, stream=f"WRB{par}")

                    def wspl(e, p=par):
                        e.tensor_copy(WSH[p][:], WRB[p][:])
                        e.drain()
                        return e.tensor_tensor(WSL[p][:], WRB[p][:], WSH[p][:], SUB)
                    P.op("DV", wspl, reads=[("WRB", par)], writes=[("WS", par)])

                    def kmm(e, d=d, p=par, first=(d == 0), last=(d == DC - 1)):
                        rh = SQH[:, d * SEQ: d * SEQ + 512]
                        rl = SQL[:, d * SEQ: d * SEQ + 512]
                        for oc in range(OC):
                            wh = WSH[p][:, oc * 128:(oc + 1) * 128]
                            wl = WSL[p][:, oc * 128:(oc + 1) * 128]
                            e.matmul(PK[oc][:], wh, rh, start=first, stop=False)
                            e.matmul(PK[oc][:], wl, rh, start=False, stop=False)
                            instr = e.matmul(PK[oc][:], wh, rl, start=False, stop=last)
                        return instr
                    P.op("PE", kmm, reads=[("WS", par), ("SQ", d)],
                         writes=[("PK", oc) for oc in range(OC)])
                for oc in range(OC):
                    P.op("AC", lambda e, oc=oc: e.activation(
                        KTMP[:], PK[oc][:], AF.Identity, bias=BKV[:, oc:oc + 1], scale=1.0),
                        reads=[("PK", oc), "BKV"], writes=["KTMP"])

                    def kspl(e, oc=oc, sc=sc):
                        dh = KH[:, oc * T + sc * 512: oc * T + sc * 512 + 512]
                        dl = KL[:, oc * T + sc * 512: oc * T + sc * 512 + 512]
                        e.tensor_copy(dh, KTMP[:])
                        e.drain()
                        return e.tensor_tensor(dl, KTMP[:], dh, SUB)
                    P.op("DV", kspl, reads=["KTMP"], writes=[("KHL", sc, oc)])
                emit_prepass(18)
                for d in range(DC):
                    par = d % 2
                    P.op("SY", lambda e, d=d, p=par: e.dma_start(
                        out=WRB[p][:], in_=wvt[d * 128:(d + 1) * 128, :]),
                        writes=[("WRB", par)], is_dma=True, stream=f"WRB{par}")

                    def wsplv(e, p=par):
                        e.tensor_copy(WSH[p][:], WRB[p][:])
                        e.drain()
                        return e.tensor_tensor(WSL[p][:], WRB[p][:], WSH[p][:], SUB)
                    P.op("DV", wsplv, reads=[("WRB", par)], writes=[("WS", par)])

                    def vmm(e, d=d, p=par, first=(d == 0), last=(d == DC - 1)):
                        for ss in range(NTT):
                            xh = SQH[:, d * SEQ + ss * 128: d * SEQ + ss * 128 + 128]
                            xl = SQL[:, d * SEQ + ss * 128: d * SEQ + ss * 128 + 128]
                            e.matmul(PV[ss][:], xh, WSH[p][:], start=first, stop=False)
                            e.matmul(PV[ss][:], xl, WSH[p][:], start=False, stop=False)
                            instr = e.matmul(PV[ss][:], xh, WSL[p][:], start=False, stop=last)
                        return instr
                    P.op("PE", vmm, reads=[("WS", par), ("SQ", d)],
                         writes=[("PV", ss) for ss in range(NTT)])
                for ss in range(NTT):
                    st = sc * NTT + ss

                    def vspl(e, ss=ss, st=st):
                        e.tensor_tensor(VTMP[:], PV[ss][:], BVB[:], ADD)
                        e.drain()
                        dh = VTH[:, st * 512:(st + 1) * 512]
                        dl = VTL[:, st * 512:(st + 1) * 512]
                        e.tensor_copy(dh, VTMP[:])
                        e.drain()
                        return e.tensor_tensor(dl, VTMP[:], dh, SUB)
                    P.op("DV", vspl, reads=[("PV", ss), "BVB"],
                         writes=[("VT", st, oc) for oc in range(OC)] + ["VTMP"])
                emit_prepass(18)
            P.barrier()

        with contextlib.ExitStack() as b3:
            tsb = lambda n, s, d: b3.enter_context(nc.sbuf_tensor(n, s, d))
            tps = lambda n, s: b3.enter_context(nc.psum_tensor(n, s, F32))
            XF3 = [tsb(f"XF3{i}", [128, NHALF], F32) for i in range(2)]
            XSH = tsb("XSH", [128, DC * NHALF], BF16)
            XSL = tsb("XSL", [128, DC * NHALF], BF16)
            WQS = [tsb("WQS0", [128, D], F32)] * 2
            WQH = tsb("WQH", [128, DC * 512], BF16)
            WQL = tsb("WQL", [128, DC * 512], BF16)
            WRED = tsb("WRED", [128, 512], F32)
            QTMP = tsb("QTMP", [128, HNT], F32)
            PQ = [tps(f"PQ{i}", [128, HNT]) for i in range(4)]
            NTTH = NHALF // 128

            for d in range(DC if UPTO >= 3 else 0):
                par = 0
                P.op("SY", lambda e, d=d: e.dma_start(
                    out=WQS[0][:], in_=wqt[d * 128:(d + 1) * 128, :]),
                    writes=[("WQS", 0)], is_dma=True, stream="WQS0")

                def wred(e, p=par, d=d):
                    for gg in range(4):
                        dst = WRED[:, gg * 128:(gg + 1) * 128]
                        sl = lambda h: WQS[p][:, (gg + 4 * h) * 128:(gg + 4 * h + 1) * 128]
                        e.tensor_tensor(dst, sl(0), sl(1), ADD)
                        e.drain()
                        e.tensor_tensor(dst, dst, sl(2), ADD)
                        e.drain()
                        e.tensor_tensor(dst, dst, sl(3), ADD)
                        e.drain()
                    e.tensor_copy(WQH[:, d * 512:(d + 1) * 512], WRED[:])
                    e.drain()
                    return e.tensor_tensor(WQL[:, d * 512:(d + 1) * 512], WRED[:],
                                           WQH[:, d * 512:(d + 1) * 512], SUB)
                P.op("DV", wred, reads=[("WQS", 0)], writes=[("WQHL", d), "WRED"])
                if d % 4 == 3:
                    emit_prepass(12)

            for half in range(2 if UPTO >= 3 else 0):
                for d in range(DC):
                    ix = half * DC + d
                    P.op("SY", lambda e, d=d, half=half, p=ix % 2: e.dma_start(
                        out=XF3[p][:], in_=x_own_t[d * 128:(d + 1) * 128,
                                                   half * NHALF:(half + 1) * NHALF]),
                        writes=[("XF3", ix % 2)], is_dma=True, stream=f"XF3{ix % 2}")

                    def spl3(e, d=d, p=ix % 2):
                        dh = XSH[:, d * NHALF:(d + 1) * NHALF]
                        dl = XSL[:, d * NHALF:(d + 1) * NHALF]
                        e.tensor_copy(dh, XF3[p][:])
                        e.drain()
                        return e.tensor_tensor(dl, XF3[p][:], dh, SUB)
                    P.op("DV", spl3, reads=[("XF3", ix % 2)], writes=[("XS", d)])
                emit_prepass(24)

                def qmm(e):
                    for d in range(DC):
                        first, last = d == 0, d == DC - 1
                        for oc in range(OC):
                            wh = WQH[:, d * 512 + oc * 128: d * 512 + (oc + 1) * 128]
                            wl = WQL[:, d * 512 + oc * 128: d * 512 + (oc + 1) * 128]
                            rh = XSH[:, d * NHALF: d * NHALF + HNT]
                            rl = XSL[:, d * NHALF: d * NHALF + HNT]
                            e.matmul(PQ[oc][:], wh, rh, start=first, stop=False)
                            e.matmul(PQ[oc][:], wl, rh, start=False, stop=False)
                            instr = e.matmul(PQ[oc][:], wh, rl, start=False, stop=last)
                    return instr
                P.op("PE", qmm,
                     reads=[("WQHL", d) for d in range(DC)]
                           + [("XS", d) for d in range(DC)],
                     writes=[("PQ", oc) for oc in range(OC)])
                for oc in range(OC):
                    P.op("AC", lambda e, oc=oc: e.activation(
                        QTMP[:], PQ[oc][:], AF.Identity, bias=BQS[:, oc:oc + 1], scale=1.0),
                        reads=[("PQ", oc), "BQS"], writes=["QTMP"])

                    def qspl(e, oc=oc, half=half):
                        o0 = oc * NTOK + half * NHALF
                        e.tensor_copy(QH[:, o0:o0 + NHALF], QTMP[:])
                        e.drain()
                        return e.tensor_tensor(QL[:, o0:o0 + NHALF], QTMP[:],
                                               QH[:, o0:o0 + NHALF], SUB)
                    P.op("DV", qspl, reads=["QTMP"], writes=[("Q", half, oc)])
            P.barrier()

        with contextlib.ExitStack() as b4:
            tsb = lambda n, s, d: b4.enter_context(nc.sbuf_tensor(n, s, d))
            tps = lambda n, s: b4.enter_context(nc.psum_tensor(n, s, F32))
            tpsb = lambda n, s: b4.enter_context(nc.psum_tensor(n, s, BF16))
            MSK = tsb("MSK", [128, NQ * 512], BF16)
            for q in range(NQ):
                ld(MSK[:, q * 512:(q + 1) * 512], masks[q], [("MSK", q)], stream="MSKS")
            S32 = tsb("S32", [128, T], F32)
            P32 = tsb("P32", [128, T], F32)
            PHh = tsb("PHh", [128, T], BF16)
            PLl = tsb("PLl", [128, T], BF16)
            RS = tsb("RS", [128, 4], F32)
            MXT = tsb("MXT", [128, 4], F32)
            NEGM = tsb("NEGM", [128, 1], F32)
            RSUM = tsb("RSUM", [128, 1], F32)
            RR = tsb("RR", [128, 1], F32)
            PHT = [tsb(f"PHT{i}", [128, 128], BF16) for i in range(2)]
            PLT = [tsb(f"PLT{i}", [128, 128], BF16) for i in range(2)]
            PSC = [tps(f"PSC{i}", [128, 512]) for i in range(2)]
            ZP = tps("ZP", [128, 512])
            PTP = [tpsb(f"PTP{i}", [128, 256]) for i in range(2)]

            for q in range(NQ if UPTO >= 4 else 0):
                ext = cfg.exts[q]
                ntile = ext // 512
                nsub = ext // 128
                half = q // (NQ // 2)
                for ti in range(ntile):
                    def smm(e, q=q, ti=ti):
                        for oc in range(OC):
                            lh = QH[:, oc * NTOK + q * 128: oc * NTOK + (q + 1) * 128]
                            ll = QL[:, oc * NTOK + q * 128: oc * NTOK + (q + 1) * 128]
                            rh = KH[:, oc * T + ti * 512: oc * T + ti * 512 + 512]
                            rl = KL[:, oc * T + ti * 512: oc * T + ti * 512 + 512]
                            e.matmul(PSC[ti % 2][:], lh, rh, start=oc == 0, stop=False)
                            e.matmul(PSC[ti % 2][:], ll, rh, start=False, stop=False)
                            instr = e.matmul(PSC[ti % 2][:], lh, rl, start=False,
                                             stop=oc == OC - 1)
                        return instr
                    P.op("PE", smm,
                         reads=[("Q", half, oc) for oc in range(OC)]
                               + [("KHL", ti, oc) for oc in range(OC)],
                         writes=[("PSC", ti % 2)])

                    def sevac(e, q=q, ti=ti, last=(ti == ntile - 1)):
                        dst = S32[:, ti * 512:(ti + 1) * 512]
                        e.tensor_copy(dst, PSC[ti % 2][:])
                        e.drain()
                        if last:
                            e.tensor_tensor(dst, dst, MSK[:, q * 512:(q + 1) * 512], ADD)
                            e.drain()
                        return e.tensor_reduce(MXT[:, ti:ti + 1], dst, AX.X, MAXOP)
                    P.op("DV", sevac, reads=[("PSC", ti % 2), ("MSK", q)],
                         writes=[("S32", ti), ("MXT", ti)])

                def stats(e, ntile=ntile):
                    if ntile == 1:
                        return e.tensor_single_scalar(NEGM[:], MXT[:, 0:1], -1.0, MUL)
                    e.tensor_tensor(NEGM[:], MXT[:, 0:1], MXT[:, 1:2], MAXOP)
                    e.drain()
                    for ti in range(2, ntile):
                        e.tensor_tensor(NEGM[:], NEGM[:], MXT[:, ti:ti + 1], MAXOP)
                        e.drain()
                    return e.tensor_single_scalar(NEGM[:], NEGM[:], -1.0, MUL)
                P.op("DV", stats, reads=[("MXT", ti) for ti in range(ntile)], writes=["NEGM"])

                for ti in range(ntile):
                    P.op("AC", lambda e, ti=ti: e.activation(
                        P32[:, ti * 512:(ti + 1) * 512], S32[:, ti * 512:(ti + 1) * 512],
                        AF.Exp, bias=NEGM[:], scale=1.0, accum_out=RS[:, ti:ti + 1]),
                        reads=[("S32", ti), "NEGM"], writes=[("P32", ti), ("RS", ti)])

                def rsum(e, ntile=ntile):
                    if ntile == 1:
                        e.tensor_copy(RSUM[:], RS[:, 0:1])
                        e.drain()
                    else:
                        e.tensor_tensor(RSUM[:], RS[:, 0:1], RS[:, 1:2], ADD)
                        e.drain()
                        for ti in range(2, ntile):
                            e.tensor_tensor(RSUM[:], RSUM[:], RS[:, ti:ti + 1], ADD)
                            e.drain()
                    e.reciprocal(RR[:], RSUM[:])
                    e.drain()
                    return e.tensor_single_scalar(RR[:], RR[:], INV_SQRT_HS, MUL)
                P.op("DV", rsum, reads=[("RS", ti) for ti in range(ntile)],
                     writes=["RR", "RSUM"])

                for ti in range(ntile):
                    def pscale(e, ti=ti):
                        sl = P32[:, ti * 512:(ti + 1) * 512]
                        e.tensor_scalar(sl, sl, RR[:], None, MUL)
                        e.drain()
                        e.tensor_copy(PHh[:, ti * 512:(ti + 1) * 512], sl)
                        e.drain()
                        return e.tensor_tensor(PLl[:, ti * 512:(ti + 1) * 512], sl,
                                               PHh[:, ti * 512:(ti + 1) * 512], SUB)
                    P.op("DV", pscale, reads=[("P32", ti), "RR"], writes=[("PHL", ti)])

                for ss in range(nsub):
                    def ptt(e, ss=ss):
                        e.transpose(PTP[ss % 2][:, 0:128],
                                    PHh[:, ss * 128:(ss + 1) * 128], IDB[:])
                        return e.transpose(PTP[ss % 2][:, 128:256],
                                           PLl[:, ss * 128:(ss + 1) * 128], IDB[:])
                    P.op("PE", ptt, reads=[("PHL", ss // 4), "IDB"], writes=[("PTP", ss % 2)])

                    def ptev(e, ss=ss):
                        e.tensor_copy(PHT[ss % 2][:], PTP[ss % 2][:, 0:128])
                        e.drain()
                        return e.tensor_copy(PLT[ss % 2][:], PTP[ss % 2][:, 128:256])
                    P.op("DV", ptev, reads=[("PTP", ss % 2)], writes=[("PHT", ss % 2)])

                    def pv(e, ss=ss, nsub=nsub):
                        for oc in range(OC):
                            vh = VTH[:, ss * 512 + oc * 128: ss * 512 + (oc + 1) * 128]
                            vl = VTL[:, ss * 512 + oc * 128: ss * 512 + (oc + 1) * 128]
                            fo = oc * 128
                            e.matmul(ZP[:, fo:fo + 128], vh, PHT[ss % 2][:],
                                     start=(ss == 0 and oc == 0), stop=False)
                            e.matmul(ZP[:, fo:fo + 128], vh, PLT[ss % 2][:],
                                     start=False, stop=False)
                            instr = e.matmul(ZP[:, fo:fo + 128], vl, PHT[ss % 2][:],
                                             start=False, stop=(ss == nsub - 1 and oc == OC - 1))
                        return instr
                    P.op("PE", pv,
                         reads=[("PHT", ss % 2)] + [("VT", ss, oc) for oc in range(OC)],
                         writes=["ZPa"])

                def zspl(e, q=q):
                    for oc in range(OC):
                        o0 = oc * NTOK + q * 128
                        e.tensor_copy(ZDH[:, o0:o0 + 128], ZP[:, oc * 128:(oc + 1) * 128])
                        e.drain()
                        e.tensor_tensor(ZDL[:, o0:o0 + 128], ZP[:, oc * 128:(oc + 1) * 128],
                                        ZDH[:, o0:o0 + 128], SUB)
                        e.drain()
                    return e.nop()
                P.op("DV", zspl, reads=["ZPa"], writes=[("ZD", q)])
                emit_prepass(42)
            emit_prepass(NSL)
            assert _pp[0] == NSL, f"prepass incomplete: {_pp[0]}/{NSL}"
            def _ppdrain(e):
                e.drain()
                return e.nop()
            P.op("AC", _ppdrain, reads=list(_last_scr.values()))
            P.barrier()

    SX = sb("SX", [128, DC * NTOK], F32)

    with contextlib.ExitStack() as b5:
        tsb = lambda n, s, d: b5.enter_context(nc.sbuf_tensor(n, s, d))
        tps = lambda n, s: b5.enter_context(nc.psum_tensor(n, s, F32))
        PSF = tsb("PSF", [128, D], F32)
        PSH = tsb("PSH", [128, OC * D], BF16)
        PSL = tsb("PSL", [128, OC * D], BF16)
        PRJ = [tsb(f"PRJ{i}", [128, D], F32) for i in range(2)]
        PO = [tps(f"PO{i}", [128, HNT]) for i in range(2)]

        for d in range(DC if UPTO >= 5 else 0):
            P.op("SY", lambda e, d=d: e.dma_start(
                out=SX[:, d * NTOK:(d + 1) * NTOK], in_=x_own_t[d * 128:(d + 1) * 128, :]),
                writes=[("SX", d, 0), ("SX", d, 1)], is_dma=True, stream=f"SXL{d % 2}")

        for gg in range(4 if UPTO >= 5 else 0):
            for r in range(4):
                i = gg * 4 + r
                P.op("SY", lambda e, row=(gg + 4 * r) * 128, p=i % 2: e.dma_start(
                    out=PRJ[p][:], in_=proj[row:row + 128, :]),
                    writes=[("PRJ", i % 2)], is_dma=True, stream=f"PRJ{i % 2}")
                if r == 0:
                    P.op("DV", lambda e, p=i % 2: e.tensor_copy(PSF[:], PRJ[p][:]),
                         reads=[("PRJ", i % 2)], writes=["PSF"])
                else:
                    P.op("DV", lambda e, p=i % 2: e.tensor_tensor(PSF[:], PSF[:], PRJ[p][:], ADD),
                         reads=[("PRJ", i % 2)], writes=["PSF"])

            def pspl(e, gg=gg):
                e.tensor_copy(PSH[:, gg * D:(gg + 1) * D], PSF[:])
                e.drain()
                return e.tensor_tensor(PSL[:, gg * D:(gg + 1) * D], PSF[:],
                                       PSH[:, gg * D:(gg + 1) * D], SUB)
            P.op("DV", pspl, reads=["PSF"], writes=[("PS", gg)])

        for half in range(2 if UPTO >= 5 else 0):
            for d in range(DC):
                def pomm(e, half=half, d=d):
                    for gg in range(4):
                        lh = PSH[:, gg * D + d * 128: gg * D + (d + 1) * 128]
                        llw = PSL[:, gg * D + d * 128: gg * D + (d + 1) * 128]
                        rh = ZDH[:, gg * NTOK + half * NHALF: gg * NTOK + half * NHALF + HNT]
                        rl = ZDL[:, gg * NTOK + half * NHALF: gg * NTOK + half * NHALF + HNT]
                        e.matmul(PO[d % 2][:], lh, rh, start=gg == 0, stop=False)
                        e.matmul(PO[d % 2][:], lh, rl, start=False, stop=False)
                        instr = e.matmul(PO[d % 2][:], llw, rh, start=False, stop=gg == 3)
                    return instr
                P.op("PE", pomm,
                     reads=[("PS", gg) for gg in range(4)]
                           + [("ZD", qq) for qq in range(half * (NQ // 2), (half + 1) * (NQ // 2))],
                     writes=[("PO", d % 2)])
                sl0 = d * NTOK + half * NHALF
                P.op("DV", lambda e, d=d, sl0=sl0: e.tensor_tensor(
                    SX[:, sl0:sl0 + NHALF], PO[d % 2][:], SX[:, sl0:sl0 + NHALF], ADD),
                    reads=[("PO", d % 2)], writes=[("SX", d, half)])
        P.barrier()

    def layernorm(tag):
        with contextlib.ExitStack() as lns:
            tsb = lambda n, s, d: lns.enter_context(nc.sbuf_tensor(n, s, d))
            tps = lambda n, s: lns.enter_context(nc.psum_tensor(n, s, F32))
            SQT = [tsb(f"SQT{tag}{i}", [128, NTOK], F32) for i in range(2)]
            NM = tsb(f"NM{tag}", [1, NTOK], F32)
            M2 = tsb(f"M2{tag}", [1, NTOK], F32)
            VAR = tsb(f"VAR{tag}", [1, NTOK], F32)
            SRT = tsb(f"SRT{tag}", [1, NTOK], F32)
            R0 = tsb(f"R0{tag}", [1, NTOK], F32)
            T1 = tsb(f"T1{tag}", [1, NTOK], F32)
            RSTD = tsb(f"RSTD{tag}", [1, NTOK], F32)
            PS1 = tps(f"PS1{tag}", [1, NTOK])
            PS2 = tps(f"PS2{tag}", [1, NTOK])
            PBM = tps(f"PBM{tag}", [128, NTOK])
            PBR = tps(f"PBR{tag}", [128, NTOK])
            NNT = NTOK // HNT
            sxk = [("SX", d, h) for d in range(DC) for h in range(2)]

            def s1mm(e):
                for d in range(DC):
                    for nt in range(NNT):
                        instr = e.matmul(PS1[:, nt * HNT:(nt + 1) * HNT], ONEC[:],
                                         SX[:, d * NTOK + nt * HNT: d * NTOK + nt * HNT + HNT],
                                         start=(d == 0 and (nt * HNT) % 512 == 0),
                                         stop=(d == DC - 1 and (((nt + 1) * HNT) % 512 == 0
                                                                or nt == NNT - 1)))
                return instr
            P.op("PE", s1mm, reads=["ONEC"] + sxk, writes=["PS1" + tag])
            for d in range(DC):
                P.op("AC", lambda e, d=d: e.activation(
                    SQT[d % 2][:], SX[:, d * NTOK:(d + 1) * NTOK], AF.Square),
                    reads=[("SX", d, 0), ("SX", d, 1)], writes=[("SQT" + tag, d % 2)])

                def s2mm(e, d=d):
                    for nt in range(NNT):
                        instr = e.matmul(PS2[:, nt * HNT:(nt + 1) * HNT], ONEC[:],
                                         SQT[d % 2][:, nt * HNT: nt * HNT + HNT],
                                         start=(d == 0 and (nt * HNT) % 512 == 0),
                                         stop=(d == DC - 1 and (((nt + 1) * HNT) % 512 == 0
                                                                or nt == NNT - 1)))
                    return instr
                P.op("PE", s2mm, reads=[("SQT" + tag, d % 2), "ONEC"], writes=["PS2" + tag])

            def small(e):
                e.tensor_single_scalar(NM[:], PS1[:], -1.0 / D, MUL)
                e.drain()
                e.tensor_tensor(M2[:], NM[:], NM[:], MUL)
                e.drain()
                e.scalar_tensor_tensor(VAR[:], PS2[:], 1.0 / D, M2[:], MUL, SUB)
                e.drain()
                return e.tensor_single_scalar(VAR[:], VAR[:], 1e-5, ADD)
            P.op("DV", small, reads=["PS1" + tag, "PS2" + tag],
                 writes=["NM" + tag, "VAR" + tag])
            P.op("AC", lambda e: e.activation(SRT[:], VAR[:], AF.Sqrt),
                 reads=["VAR" + tag], writes=["SRT" + tag])

            def newt(e):
                e.reciprocal(R0[:], SRT[:])
                e.drain()
                e.tensor_tensor(T1[:], R0[:], R0[:], MUL)
                e.drain()
                e.tensor_tensor(T1[:], T1[:], VAR[:], MUL)
                e.drain()
                e.tensor_scalar(T1[:], T1[:], -0.5, 1.5, MUL, ADD)
                e.drain()
                return e.tensor_tensor(RSTD[:], R0[:], T1[:], MUL)
            P.op("DV", newt, reads=["SRT" + tag, "VAR" + tag], writes=["RSTD" + tag])

            def bmm(e):
                for nt in range(NNT):
                    st_ = (nt * HNT) % 512 == 0
                    sp_ = ((nt + 1) * HNT) % 512 == 0 or nt == NNT - 1
                    e.matmul(PBM[:, nt * HNT:(nt + 1) * HNT], ONER[:],
                             NM[:, nt * HNT: nt * HNT + HNT], start=st_, stop=sp_)
                    instr = e.matmul(PBR[:, nt * HNT:(nt + 1) * HNT], ONER[:],
                                     RSTD[:, nt * HNT: nt * HNT + HNT], start=st_, stop=sp_)
                return instr
            P.op("PE", bmm, reads=["NM" + tag, "RSTD" + tag, "ONER"],
                 writes=["PBM" + tag, "PBR" + tag])

            for d in range(DC):
                def apply(e, d=d):
                    for h in range(2):
                        sl = SX[:, d * NTOK + h * NHALF: d * NTOK + h * NHALF + NHALF]
                        e.tensor_tensor(sl, sl, PBM[:, h * NHALF:(h + 1) * NHALF], ADD)
                        e.drain()
                        e.tensor_tensor(sl, sl, PBR[:, h * NHALF:(h + 1) * NHALF], MUL)
                        e.drain()
                    return e.nop()
                P.op("DV", apply, reads=["PBM" + tag, "PBR" + tag],
                     writes=[("SX", d, 0), ("SX", d, 1)])
            P.barrier()

    if UPTO >= 6:
        layernorm("L1")

    with contextlib.ExitStack() as b7:
        tsb = lambda n, s, d: b7.enter_context(nc.sbuf_tensor(n, s, d))
        tps = lambda n, s: b7.enter_context(nc.psum_tensor(n, s, F32))
        XSH7 = tsb("XSH7", [128, DC * NHALF], BF16)
        XSL7 = tsb("XSL7", [128, DC * NHALF], BF16)
        QG = [tsb(f"QG{i}", [128, DC * G], BF16) for i in range(2)]
        QU = [tsb(f"QU{i}", [128, DC * G], BF16) for i in range(2)]
        QD = tsb("QD", [128, GKC * D], BF16)
        G1 = tsb("G1", [128, HNT], F32)
        G1S = tsb("G1S", [128, HNT], F32)
        U1 = tsb("U1", [128, HNT], F32)
        FT = tsb("FT", [128, HNT], F32)
        FH = tsb("FH", [128, GKC * HNT], BF16)
        FL = tsb("FL", [128, GKC * HNT], BF16)
        PG = [tps(f"PG{i}", [128, HNT]) for i in range(2)]
        PU = [tps(f"PU{i}", [128, HNT]) for i in range(2)]
        PD = [tps(f"PD{i}", [128, HNT]) for i in range(2)]

        for half in range(2 if UPTO >= 7 else 0):
            for d in range(DC):
                def xspl(e, d=d, half=half):
                    src = SX[:, d * NTOK + half * NHALF: d * NTOK + half * NHALF + NHALF]
                    e.tensor_copy(XSH7[:, d * NHALF:(d + 1) * NHALF], src)
                    e.drain()
                    return e.tensor_tensor(XSL7[:, d * NHALF:(d + 1) * NHALF], src,
                                           XSH7[:, d * NHALF:(d + 1) * NHALF], SUB)
                P.op("DV", xspl, reads=[("SX", d, half)], writes=[("XS7", d)])

            for grp in range(NGRP):
                par = grp % 2
                if half == 0:
                    ensure_prepass(GRP_BOUND[grp])
                for kc in range(DC):
                    P.op("SY", lambda e, kc=kc, grp=grp, p=par: e.dma_start(
                        out=QG[p][:, kc * G:(kc + 1) * G],
                        in_=sc_g[kc][:, grp * G:(grp + 1) * G]),
                        reads=[("scr", "g", kc, w_) for w_ in
                               range(grp * G // W, min(I // W, ((grp + 1) * G + W - 1) // W))],
                        writes=[("QG", par, kc)], is_dma=True, stream=f"QG{par}")
                    P.op("SY", lambda e, kc=kc, grp=grp, p=par: e.dma_start(
                        out=QU[p][:, kc * G:(kc + 1) * G],
                        in_=sc_u[kc][:, grp * G:(grp + 1) * G]),
                        reads=[("scr", "u", kc, w_) for w_ in
                               range(grp * G // W, min(I // W, ((grp + 1) * G + W - 1) // W))],
                        writes=[("QU", par, kc)], is_dma=True, stream=f"QU{par}")
                for kc in range(GKC):
                    icg = grp * GKC + kc
                    P.op("SY", lambda e, icg=icg, kc=kc: e.dma_start(
                        out=QD[:, kc * D:(kc + 1) * D], in_=sc_d[icg][:, :]),
                        reads=[("scr", "d", icg, w_) for w_ in range(D // W)],
                        writes=[("QD", kc)], is_dma=True, stream="QD")

                for kc in range(GKC):
                    icg = grp * GKC + kc
                    for MAT, PM, QW, SCL, outf, outk in (
                            ("G", PG, QG, GSC, G1, "G1"), ("U", PU, QU, USC, U1, "U1")):
                        def gmm(e, kc=kc, p=par, PM=PM, QW=QW):
                            for d in range(DC):
                                lw = QW[p][:, d * G + kc * 128: d * G + kc * 128 + 128]
                                e.matmul(PM[kc % 2][:], lw, XSH7[:, d * NHALF: d * NHALF + HNT],
                                         start=d == 0, stop=False)
                                instr = e.matmul(PM[kc % 2][:], lw,
                                                 XSL7[:, d * NHALF: d * NHALF + HNT],
                                                 start=False, stop=d == DC - 1)
                            return instr
                        P.op("PE", gmm,
                             reads=[("Q" + MAT, par, d) for d in range(DC)]
                                   + [("XS7", d) for d in range(DC)],
                             writes=[("P" + MAT, kc % 2)])
                        P.op("DV", lambda e, kc=kc, icg=icg, PM=PM, SCL=SCL, outf=outf:
                             e.tensor_scalar(outf[:], PM[kc % 2][:], SCL[:, icg:icg + 1],
                                             None, MUL),
                             reads=[("P" + MAT, kc % 2),
                                    "GSC" if MAT == "G" else "USC"],
                             writes=[outk])
                    P.op("AC", lambda e: e.activation(G1S[:], G1[:], AF.Silu),
                         reads=["G1"], writes=["G1S"])

                    def fuse(e, kc=kc):
                        e.tensor_tensor(FT[:], G1S[:], U1[:], MUL)
                        e.drain()
                        e.tensor_copy(FH[:, kc * HNT:(kc + 1) * HNT], FT[:])
                        e.drain()
                        return e.tensor_tensor(FL[:, kc * HNT:(kc + 1) * HNT], FT[:],
                                               FH[:, kc * HNT:(kc + 1) * HNT], SUB)
                    P.op("DV", fuse, reads=["G1S", "U1"], writes=[("F", kc), "FT"])

                for h in range(DC):
                    def dmm(e, h=h):
                        for kc in range(GKC):
                            lw = QD[:, kc * D + h * 128: kc * D + (h + 1) * 128]
                            e.matmul(PD[h % 2][:], lw, FH[:, kc * HNT:(kc + 1) * HNT],
                                     start=kc == 0, stop=False)
                            instr = e.matmul(PD[h % 2][:], lw, FL[:, kc * HNT:(kc + 1) * HNT],
                                             start=False, stop=kc == GKC - 1)
                        return instr
                    P.op("PE", dmm,
                         reads=[("QD", kc) for kc in range(GKC)]
                               + [("F", kc) for kc in range(GKC)],
                         writes=[("PD", h % 2)])
                    sl0 = h * NTOK + half * NHALF
                    P.op("DV", lambda e, h=h, sl0=sl0: e.scalar_tensor_tensor(
                        SX[:, sl0:sl0 + NHALF], PD[h % 2][:], DSC[:, h:h + 1],
                        SX[:, sl0:sl0 + NHALF], MUL, ADD),
                        reads=[("PD", h % 2), "DSC"], writes=[("SX", h, half)])
        P.barrier()



    if UPTO >= 8:
        layernorm("L2")

    if UPTO >= 9:
        for d in range(DC):
            P.op("GP", lambda e, d=d: e.dma_start(
                out=y_t[d * 128:(d + 1) * 128, :], in_=SX[:, d * NTOK:(d + 1) * NTOK]),
                reads=[("SX", d, 0), ("SX", d, 1)],
                writes=[("ydone", d)], is_dma=True, stream="YOUT")

        def _youtw(e):
            e.drain()
            return e.nop()
        P.op("GP", _youtw, reads=[("ydone", d) for d in range(DC)])

    P.lower(nc, es)
    with nc.Block() as blk:
        @blk.tensor
        def _(eng):
            P.emit_engine("PE", eng)

        @blk.vector
        def _(eng):
            P.emit_engine("DV", eng)

        @blk.scalar
        def _(eng):
            P.emit_engine("AC", eng)

        @blk.gpsimd
        def _(eng):
            P.emit_engine("GP", eng)

        @blk.sync
        def _(eng):
            P.emit_engine("SY", eng)

    es.close()
    return nc


def make_core_inputs(cfg, X, WQ, bQ, WK, bK, WV, bV, PRJ, GW, UW, DW, GS, US, DS):
    NQ = cfg.NQ
    wqt = np.ascontiguousarray(WQ.T)
    wkt = np.ascontiguousarray(WK.T)
    wvt = np.ascontiguousarray(WV.T)
    prj = np.ascontiguousarray(PRJ)
    gi = np.ascontiguousarray(GW).view(np.int32)
    ui = np.ascontiguousarray(UW).view(np.int32)
    di = np.ascontiguousarray(DW).view(np.int32)
    bq128 = np.ascontiguousarray(bQ.reshape(16, 128).T)
    bkv = np.ascontiguousarray(
        np.concatenate([bK.reshape(4, 128).T, bV.reshape(4, 128).T], axis=1))
    gsc = np.ascontiguousarray(GS.reshape(cfg.IC, 128).T)
    usc = np.ascontiguousarray(US.reshape(cfg.IC, 128).T)
    dsc = np.ascontiguousarray(DS.reshape(cfg.DC, 128).T)
    eyef = np.eye(128, dtype=np.float32)
    eyeb = np.eye(128, dtype=np.float32).astype(ml_dtypes.bfloat16)

    maps = []
    for core in range(cfg.ncores):
        b, m = core // 2, core % 2
        chunks = cfg.chunks[m]
        rows = np.concatenate([np.arange(j * 128, (j + 1) * 128) for j in chunks])
        mk = np.zeros((NQ, 128, 512), dtype=np.float32)
        for lq, j in enumerate(chunks):
            s0 = cfg.exts[lq] - 512
            col = s0 + np.arange(512)[None, :]
            row = j * 128 + np.arange(128)[:, None]
            mk[lq] = np.where(col <= row, 0.0, -30000.0)
        maps.append({
            "x_seq_t": np.ascontiguousarray(X[b].T),
            "x_own_t": np.ascontiguousarray(X[b][rows].T),
            "wqt": wqt, "wkt": wkt, "wvt": wvt, "proj": prj,
            "gate_i": gi, "up_i": ui, "down_i": di,
            "bq": bq128, "bkv": bkv, "bvr": np.ascontiguousarray(bV[None, :]),
            "gsc": gsc, "usc": usc, "dsc": dsc,
            "masks": mk.astype(ml_dtypes.bfloat16),
            "eyef": eyef, "eyeb": eyeb,
        })
    return maps


def assemble_output(cfg, results):
    Y = np.zeros((cfg.B, cfg.T, cfg.D), dtype=np.float32)
    for core, res in enumerate(results):
        b, m = core // 2, core % 2
        yc = np.ascontiguousarray(np.asarray(res["y_t"]).T)
        for lq, j in enumerate(cfg.chunks[m]):
            Y[b, j * 128:(j + 1) * 128] = yc[lq * 128:(lq + 1) * 128]
    return Y


def kernel(L0_attn_x, L0_attn_WQ, L0_attn_bQ, L0_attn_WK, L0_attn_bK,
           L0_attn_WV, L0_attn_bV, L0_attn_proj, L0_ffn_gate_w, L0_ffn_up_w,
           L0_ffn_down_w, L0_ffn_gate_scale, L0_ffn_up_scale, L0_ffn_down_scale):
    cfg = Cfg(B=4, T=2048, D=2048, I=6144, igroup=384, prew=512)
    X = np.asarray(L0_attn_x, dtype=np.float32)
    maps = make_core_inputs(
        cfg, X,
        np.asarray(L0_attn_WQ, np.float32), np.asarray(L0_attn_bQ, np.float32),
        np.asarray(L0_attn_WK, np.float32), np.asarray(L0_attn_bK, np.float32),
        np.asarray(L0_attn_WV, np.float32), np.asarray(L0_attn_bV, np.float32),
        np.asarray(L0_attn_proj, np.float32),
        np.asarray(L0_ffn_gate_w, np.float32), np.asarray(L0_ffn_up_w, np.float32),
        np.asarray(L0_ffn_down_w, np.float32),
        np.asarray(L0_ffn_gate_scale, np.float32), np.asarray(L0_ffn_up_scale, np.float32),
        np.asarray(L0_ffn_down_scale, np.float32))
    nc = build_program(cfg)
    res = run_bass_kernel_spmd(nc, maps, core_ids=list(range(cfg.ncores)))
    return assemble_output(cfg, res.results)
